# version 11
# speedup vs baseline: 1.0071x; 1.0066x over previous
import contextlib
import numpy as np
import ml_dtypes
import concourse.bass as bass
import concourse.mybir as mybir
from concourse.bass_utils import run_bass_kernel_spmd

F32 = mybir.dt.float32
BF16 = mybir.dt.bfloat16
I32 = mybir.dt.int32
AF = mybir.ActivationFunctionType
ALU = mybir.AluOpType
AX = mybir.AxisListType
BF = ml_dtypes.bfloat16

D = 1024
TSEQ = 8192
NB = 2
TOK = 16384
NCORE = 8
CH = 2048
DFF = 4096
EXP = 8
INV_D = 1.0 / D
LN_EPS = 1e-5
TWO_PI = float(2.0 * np.pi)
INV_2PI = float(1.0 / (2.0 * np.pi))
E20 = float(np.exp(np.float32(-20.0)))
PIH = float(np.pi / 2)
PI_F = float(np.pi)
INV2PI = float(1.0 / (2.0 * np.pi))


class Sem:
    def __init__(self, ctx, nc, name):
        self.h = ctx.enter_context(nc.semaphore(name))
        self.n = 0

    def tick(self, instr, k=1):
        instr.then_inc(self.h, k)
        self.n += k
        return self.n



def emit_ln(block, sq, st, sv, ss, x_tile, xn_bf_tile, xn_f_tile=None):
    def _(v):
        v.reduce_sum(st[:, 0:1], x_tile, axis=AX.X)
        v.tensor_tensor(sq[:], x_tile, x_tile, ALU.mult)
        v.drain()
        v.reduce_sum(st[:, 2:3], sq[:], axis=AX.X)
        v.drain()
        v.tensor_scalar(st[:, 1:2], st[:, 0:1], INV_D, None, ALU.mult)
        v.drain()
        v.tensor_tensor(st[:, 3:4], st[:, 1:2], st[:, 1:2], ALU.mult)
        v.tensor_scalar(st[:, 4:5], st[:, 2:3], INV_D, LN_EPS, ALU.mult, ALU.add)
        v.drain()
        sv.tick(v.tensor_tensor(st[:, 4:5], st[:, 4:5], st[:, 3:4], ALU.subtract))
    block.vector(_)
    vdone = sv.n

    def _(s):
        s.wait_ge(sv.h, vdone)
        s.activation(st[:, 5:6], st[:, 4:5], AF.Ln)
        s.drain()
        ss.tick(s.activation(st[:, 5:6], st[:, 5:6], AF.Exp, scale=-0.5))
    block.scalar(_)
    sdone = ss.n

    def _(v):
        v.wait_ge(ss.h, sdone)
        v.scalar_tensor_tensor(st[:, 6:7], st[:, 1:2], -1.0, st[:, 5:6], ALU.mult, ALU.mult)
        v.drain()
        ins = v.tensor_scalar(xn_bf_tile, x_tile, st[:, 5:6], st[:, 6:7], ALU.mult, ALU.add)
        if xn_f_tile is not None:
            ins = v.tensor_scalar(xn_f_tile, x_tile, st[:, 5:6], st[:, 6:7], ALU.mult, ALU.add)
        sv.tick(ins)
    block.vector(_)
    return sv.n


def emit_kround(v, zin, zout, scr_f, scr_i, scr_g=None, small=False):
    assert scr_g is not None
    v.tensor_scalar(scr_f, zin, INV_2PI, None, ALU.mult)
    v.drain()
    v.tensor_copy(scr_i, scr_f)
    v.drain()
    v.tensor_copy(scr_f, scr_i)
    v.drain()
    v.scalar_tensor_tensor(zout, scr_f, -TWO_PI, zin, ALU.mult, ALU.add)
    v.drain()
    v.tensor_scalar(scr_f, zout, float(-np.pi), None, ALU.is_lt)
    v.tensor_scalar(scr_g, zout, float(np.pi), None, ALU.is_gt)
    v.drain()
    v.tensor_tensor(scr_f, scr_f, scr_g, ALU.subtract)
    v.drain()
    v.scalar_tensor_tensor(zout, scr_f, TWO_PI, zout, ALU.mult, ALU.add)
    v.drain()



def build_b1():
    nc = bass.Bass("TRN2", target_bir_lowering=False, debug=False)
    x0 = nc.dram_tensor("x0c", [CH, D], F32, kind="ExternalInput").ap()
    ident = nc.dram_tensor("ident", [128, 128], F32, kind="ExternalInput").ap()
    xnT = nc.dram_tensor("xnT", [D, CH], BF16, kind="ExternalOutput").ap()

    NT = CH // 128
    with contextlib.ExitStack() as ctx:
        E = ctx.enter_context
        xt = [E(nc.sbuf_tensor(f"xt{i}", [128, D], F32)) for i in range(2)]
        xn = [E(nc.sbuf_tensor(f"xn{i}", [128, D], BF16)) for i in range(2)]
        sq = E(nc.sbuf_tensor("sq", [128, D], F32))
        st = E(nc.sbuf_tensor("st", [128, 8], F32))
        idb = E(nc.sbuf_tensor("idb", [128, 128], BF16))
        xnT_sb = E(nc.sbuf_tensor("xnT_sb", [128, 8 * CH], BF16))
        pst = [E(nc.psum_tensor(f"pst{i}", [128, 1024], BF16)) for i in range(2)]
        sy = Sem(ctx, nc, "sy")
        syp = [Sem(ctx, nc, f"syp{i}") for i in range(2)]
        gp = Sem(ctx, nc, "gp")
        sv = Sem(ctx, nc, "sv")
        ss = Sem(ctx, nc, "ss")
        sm = Sem(ctx, nc, "sm")
        so = Sem(ctx, nc, "so")
        block = E(nc.Block())

        def _(g):
            gp.tick(g.dma_start(out=idb[:], in_=ident), 16)
        block.gpsimd(_)

        tp_mark = {}
        cp_mark = {}
        for tt in range(NT):
            pp = tt % 2

            def _(s, tt=tt, pp=pp):
                if tt >= 2:
                    s.wait_ge(sm.h, tp_mark[tt - 2])
                syp[pp].tick(s.dma_start(out=xt[pp][:], in_=x0[tt * 128:(tt + 1) * 128, :]), 16)
            block.sync(_)
            ld = syp[pp].n

            def _(v, pp=pp, ld=ld):
                v.wait_ge(syp[pp].h, ld)
            block.vector(_)
            ln_done = emit_ln(block, sq, st, sv, ss, xt[pp][:], xn[pp][:])

            def _(t, tt=tt, pp=pp, ln_done=ln_done):
                t.wait_ge(sv.h, ln_done)
                t.wait_ge(gp.h, 16)
                if tt >= 2:
                    t.wait_ge(so.h, cp_mark[tt - 2])
                for dt in range(8):
                    ins = t.transpose(pst[pp][:, dt * 128:(dt + 1) * 128],
                                      xn[pp][:, dt * 128:(dt + 1) * 128], idb[:])
                tp_mark[tt] = sm.tick(ins)
            block.tensor(_)

            def _(v, tt=tt, pp=pp):
                v.wait_ge(sm.h, tp_mark[tt])
                dst = xnT_sb[:].rearrange("p (a t) -> p a t", a=8)[:, :, tt * 128:(tt + 1) * 128]
                cp_mark[tt] = so.tick(v.tensor_copy(dst, pst[pp][:].rearrange("p (a t) -> p a t", a=8)))
            block.vector(_)

        def _(s):
            s.wait_ge(so.h, cp_mark[NT - 1])
            sy.tick(s.dma_start(out=xnT.rearrange("(a p) t -> p a t", p=128),
                                in_=xnT_sb[:].rearrange("p (a t) -> p a t", a=8)), 16)
            s.wait_ge(sy.h, sy.n)
        block.sync(_)
    return nc



def build_b2():
    nc = bass.Bass("TRN2", target_bir_lowering=False, debug=False)
    xnT = nc.dram_tensor("xnT_full", [D, TOK], BF16, kind="ExternalInput").ap()
    wq = nc.dram_tensor("wqT", [D, 128], F32, kind="ExternalInput").ap()
    wk = nc.dram_tensor("wkT", [D, 128], F32, kind="ExternalInput").ap()
    wv = nc.dram_tensor("wvT", [D, 128], F32, kind="ExternalInput").ap()
    tri = nc.dram_tensor("tri", [128, 4 * 512], F32, kind="ExternalInput").ap()
    ident = nc.dram_tensor("ident", [128, 128], F32, kind="ExternalInput").ap()
    aoT = nc.dram_tensor("aoT", [128, TOK], BF16, kind="ExternalOutput").ap()

    NQC = TSEQ // 512
    with contextlib.ExitStack() as ctx:
        E = ctx.enter_context
        wq_sb = E(nc.sbuf_tensor("wq_sb", [128, 8 * 128], BF16))
        wk_sb = E(nc.sbuf_tensor("wk_sb", [128, 8 * 128], BF16))
        wv_sb = E(nc.sbuf_tensor("wv_sb", [128, 8 * 128], BF16))
        tri_sb = E(nc.sbuf_tensor("tri_sb", [128, 4 * 512], BF16))
        idb = E(nc.sbuf_tensor("idb", [128, 128], BF16))
        QT = E(nc.sbuf_tensor("QT", [128, TSEQ], BF16))
        KT = E(nc.sbuf_tensor("KT", [128, TSEQ], BF16))
        VO = E(nc.sbuf_tensor("VO", [128, 64 * 130], BF16))
        VT_t = E(nc.sbuf_tensor("VT_t", [128, 512], BF16))
        xin = [E(nc.sbuf_tensor(f"xin{i}", [128, 8 * 512], BF16)) for i in range(2)]
        expb = [E(nc.sbuf_tensor(f"expb{i}", [128, 512], BF16)) for i in range(2)]
        rcp = E(nc.sbuf_tensor("rcp", [128, 4], F32))
        m10 = E(nc.sbuf_tensor("m10", [128, 1], F32))
        anat = E(nc.sbuf_tensor("anat", [128, 4 * 64], BF16))
        ao_h = [E(nc.sbuf_tensor(f"ao_h{h}", [64, TOK], BF16)) for h in range(2)]
        psq = E(nc.psum_tensor("psq", [128, 512], F32))
        psk = E(nc.psum_tensor("psk", [128, 512], F32))
        psv = E(nc.psum_tensor("psv", [128, 512], F32))
        pstp = E(nc.psum_tensor("pstp", [128, 512], BF16))
        scps = [E(nc.psum_tensor(f"scps{i}", [128, 512], F32)) for i in range(2)]
        avp = E(nc.psum_tensor("avp", [128, 260], F32))
        sy = Sem(ctx, nc, "sy")
        sxin = [Sem(ctx, nc, f"sxin{i}") for i in range(2)]
        gp = Sem(ctx, nc, "gp")
        gm = Sem(ctx, nc, "gm")
        smQ = Sem(ctx, nc, "smQ")
        smS = Sem(ctx, nc, "smS")
        smA = Sem(ctx, nc, "smA")
        smT = Sem(ctx, nc, "smT")
        sE = Sem(ctx, nc, "sE")
        sv = Sem(ctx, nc, "sv")
        svm = Sem(ctx, nc, "svm")
        svn = Sem(ctx, nc, "svn")
        block = E(nc.Block())

        def _(g):
            for dst, src in ((wq_sb, wq), (wk_sb, wk), (wv_sb, wv)):
                g.dma_start(out=dst[:].rearrange("p (a m) -> p a m", a=8),
                            in_=src.rearrange("(a p) m -> p a m", p=128)).then_inc(gp.h, 16)
            g.dma_start(out=tri_sb[:], in_=tri).then_inc(gp.h, 16)
            g.dma_start(out=idb[:], in_=ident).then_inc(gp.h, 16)
            gp.n += 5 * 16
            g.memset(VO[:], 1.0)
            g.memset(m10[:], -10.0)
            gm.tick(g.engine_nop())
        block.gpsimd(_)
        w_ready = gp.n

        for b in range(NB):
            ld_mark = {}
            vev_mark = {}
            q_mark = {}
            k_mark = {}
            tp_mark = {}
            vo_mark = {}
            mmv_mark = {}
            mmqk_mark = {}
            attnS_prev = smS.n
            attnA_prev = smA.n
            svn_prev = svn.n
            smQ_prev = smQ.n
            for nn in range(NQC):
                pp = nn % 2

                def _(s, nn=nn, pp=pp):
                    if nn >= 2:
                        s.wait_ge(smQ.h, mmqk_mark[nn - 2])
                    else:
                        s.wait_ge(smQ.h, smQ_prev)
                    ld_mark[nn] = sxin[pp].tick(s.dma_start(
                        out=xin[pp][:].rearrange("p (a t) -> p a t", a=8),
                        in_=xnT[:, b * TSEQ + nn * 512: b * TSEQ + (nn + 1) * 512]
                        .rearrange("(a p) t -> p a t", p=128)), 16)
                block.sync(_)

                def _(t, nn=nn, pp=pp):
                    t.wait_ge(sxin[pp].h, ld_mark[nn])
                    t.wait_ge(gp.h, w_ready)
                    t.wait_ge(gm.h, 1)
                    if nn >= 1:
                        t.wait_ge(sv.h, vev_mark[nn - 1])
                    for kt in range(8):
                        ins = t.matmul(psv[:, :], wv_sb[:, kt * 128:(kt + 1) * 128],
                                       xin[pp][:, kt * 512:(kt + 1) * 512],
                                       start=(kt == 0), stop=(kt == 7))
                    mmv_mark[nn] = smQ.tick(ins)
                    if nn >= 1:
                        t.wait_ge(sv.h, q_mark[nn - 1])
                        t.wait_ge(sE.h, k_mark[nn - 1])
                    for kt in range(8):
                        t.matmul(psq[:, :], wq_sb[:, kt * 128:(kt + 1) * 128],
                                 xin[pp][:, kt * 512:(kt + 1) * 512],
                                 start=(kt == 0), stop=(kt == 7))
                    for kt in range(8):
                        ins = t.matmul(psk[:, :], wk_sb[:, kt * 128:(kt + 1) * 128],
                                       xin[pp][:, kt * 512:(kt + 1) * 512],
                                       start=(kt == 0), stop=(kt == 7))
                    mmqk_mark[nn] = smQ.tick(ins)
                block.tensor(_)

                def _(v, nn=nn):
                    v.wait_ge(smQ.h, mmv_mark[nn])
                    if nn >= 1:
                        v.wait_ge(smT.h, tp_mark[nn - 1])
                    vev_mark[nn] = sv.tick(v.tensor_copy(VT_t[:], psv[:, :]))
                    if nn == 0:
                        v.wait_ge(smS.h, attnS_prev)
                    v.wait_ge(smQ.h, mmqk_mark[nn])
                    q_mark[nn] = sv.tick(v.tensor_scalar(QT[:, nn * 512:(nn + 1) * 512],
                                                         psq[:, :], 0.125, None, ALU.mult))
                block.vector(_)

                def _(s, nn=nn):
                    if nn == 0:
                        s.wait_ge(smS.h, attnS_prev)
                    s.wait_ge(smQ.h, mmqk_mark[nn])
                    k_mark[nn] = sE.tick(s.activation(KT[:, nn * 512:(nn + 1) * 512],
                                                      psk[:, :], AF.Copy))
                block.scalar(_)

                def _(t, nn=nn):
                    t.wait_ge(sv.h, vev_mark[nn])
                    if nn >= 1:
                        t.wait_ge(sv.h, vo_mark[nn - 1])
                    else:
                        t.wait_ge(svn.h, svn_prev)
                    for s4 in range(4):
                        ins = t.transpose(pstp[:, s4 * 128:(s4 + 1) * 128],
                                          VT_t[:, s4 * 128:(s4 + 1) * 128], idb[:])
                    tp_mark[nn] = smT.tick(ins)
                block.tensor(_)

                def _(v, nn=nn):
                    v.wait_ge(smT.h, tp_mark[nn])
                    if nn == 0:
                        v.wait_ge(smA.h, attnA_prev)
                    for s4 in range(4):
                        kk = nn * 4 + s4
                        ins = v.tensor_copy(
                            VO[:, kk * 130: kk * 130 + 130].rearrange("p (h c) -> p h c", h=2)[:, :, 0:64],
                            pstp[:, s4 * 128:(s4 + 1) * 128].rearrange("p (h c) -> p h c", h=2))
                    vo_mark[nn] = sv.tick(ins)
                block.vector(_)

            qkv_v = sv.n
            qkv_s = sE.n

            for h in range(2):
                hb = 64 * h
                for qc in range(NQC):
                    nkk = 4 * qc + 4
                    sc0 = smS.n
                    av0 = smA.n
                    ex0 = sE.n
                    cl0 = svm.n
                    nrm_prev = svn.n

                    def _(t, h=h, qc=qc, hb=hb, nkk=nkk, sc0=sc0, ex0=ex0,
                          cl0=cl0, nrm_prev=nrm_prev):
                        t.wait_ge(sv.h, qkv_v)
                        t.wait_ge(sE.h, qkv_s)
                        t.wait_ge(svn.h, nrm_prev)
                        for kk in range(nkk + 1):
                            if kk < nkk:
                                if kk >= 2:
                                    t.wait_ge(sE.h, ex0 + kk - 1)
                                ins = t.matmul(scps[kk % 2][:, :],
                                               KT[hb:hb + 64, kk * 128:(kk + 1) * 128],
                                               QT[hb:hb + 64, qc * 512:(qc + 1) * 512],
                                               start=True, stop=True)
                                smS.tick(ins)
                            if kk >= 1:
                                ak = kk - 1
                                t.wait_ge(svm.h, cl0 + ak + 1)
                                last = None
                                for s4 in range(4):
                                    if ak <= 4 * qc + s4:
                                        last = t.matmul(avp[:, s4 * 65:(s4 + 1) * 65],
                                                        expb[ak % 2][:, s4 * 128:(s4 + 1) * 128],
                                                        VO[:, ak * 130 + 65 * h: ak * 130 + 65 * h + 65],
                                                        start=(ak == 0 and s4 == 0),
                                                        stop=(ak == nkk - 1 and s4 == 3))
                                smA.tick(last)
                    block.tensor(_)

                    def _(s, nkk=nkk, sc0=sc0, av0=av0):
                        for kk in range(nkk):
                            s.wait_ge(smS.h, sc0 + kk + 1)
                            if kk >= 2:
                                s.wait_ge(smA.h, av0 + kk - 1)
                            sE.tick(s.activation(expb[kk % 2][:], scps[kk % 2][:, :],
                                                 AF.Exp, bias=m10[:, 0:1]))
                    block.scalar(_)

                    def _(v, qc=qc, nkk=nkk, ex0=ex0):
                        for kk in range(nkk):
                            v.wait_ge(sE.h, ex0 + kk + 1)
                            v.tensor_scalar(expb[kk % 2][:], expb[kk % 2][:],
                                            1.0, E20, ALU.min, ALU.max)
                            vd = kk - 4 * qc
                            if vd >= 0:
                                v.drain()
                                v.tensor_tensor(expb[kk % 2][:], expb[kk % 2][:],
                                                tri_sb[:, vd * 512:(vd + 1) * 512], ALU.mult)
                            svm.tick(v.engine_nop())
                    block.vector(_)

                    av_all = av0 + nkk

                    def _(v, av_all=av_all):
                        v.wait_ge(smA.h, av_all)
                        for s4 in range(4):
                            v.reciprocal(rcp[:, s4:s4 + 1], avp[:, s4 * 65 + 64: s4 * 65 + 65])
                        v.drain()
                        for s4 in range(4):
                            ins = v.tensor_scalar(anat[:, s4 * 64:(s4 + 1) * 64],
                                                  avp[:, s4 * 65:s4 * 65 + 64],
                                                  rcp[:, s4:s4 + 1], None, ALU.mult)
                        sv.tick(ins)
                    block.vector(_)
                    scaled = sv.n

                    def _(t, scaled=scaled):
                        t.wait_ge(sv.h, scaled)
                        for s4 in range(4):
                            ins = t.transpose(pstp[0:64, s4 * 128:(s4 + 1) * 128],
                                              anat[:, s4 * 64:(s4 + 1) * 64], idb[:])
                        smT.tick(ins)
                    block.tensor(_)
                    tpd = smT.n

                    def _(v, b=b, h=h, qc=qc, tpd=tpd):
                        v.wait_ge(smT.h, tpd)
                        base = b * TSEQ + qc * 512
                        ins = v.tensor_copy(
                            ao_h[h][0:64, base:base + 512].rearrange("p (s c) -> p s c", s=4),
                            pstp[0:64, :].rearrange("p (s c) -> p s c", s=4))
                        svn.tick(ins)
                    block.vector(_)

        def _(s):
            s.wait_ge(svn.h, svn.n)
            s.dma_start(out=aoT[0:64, :], in_=ao_h[0][0:64, :]).then_inc(sy.h, 16)
            s.dma_start(out=aoT[64:128, :], in_=ao_h[1][0:64, :]).then_inc(sy.h, 16)
            sy.n += 32
            s.wait_ge(sy.h, sy.n)
        block.sync(_)
    return nc



def build_b3():
    nc = bass.Bass("TRN2", target_bir_lowering=False, debug=False)
    x0 = nc.dram_tensor("x0c", [CH, D], F32, kind="ExternalInput").ap()
    aoTc = nc.dram_tensor("aoTc", [D, CH], BF16, kind="ExternalInput").ap()
    woutT = nc.dram_tensor("woutT", [D, D], F32, kind="ExternalInput").ap()
    xnTc = nc.dram_tensor("xnTc", [D, CH], BF16, kind="ExternalInput").ap()
    W1 = nc.dram_tensor("W1", [D, DFF], F32, kind="ExternalInput").ap()
    W2 = nc.dram_tensor("W2", [DFF, D], F32, kind="ExternalInput").ap()
    b1v = nc.dram_tensor("b1v", [128, 32], F32, kind="ExternalInput").ap()
    thv = nc.dram_tensor("thv", [128, 32], F32, kind="ExternalInput").ap()
    wrv = nc.dram_tensor("wrv", [128, 32], F32, kind="ExternalInput").ap()
    wiv = nc.dram_tensor("wiv", [128, 32], F32, kind="ExternalInput").ap()
    b2row = nc.dram_tensor("b2row", [1, D], F32, kind="ExternalInput").ap()
    rw = nc.dram_tensor("rw", [D, 8], F32, kind="ExternalInput").ap()
    rbrow = nc.dram_tensor("rbrow", [1, 8], F32, kind="ExternalInput").ap()
    ident = nc.dram_tensor("ident", [128, 128], F32, kind="ExternalInput").ap()
    x1c = nc.dram_tensor("x1c", [CH, D], F32, kind="ExternalOutput").ap()
    xn1T = nc.dram_tensor("xn1T", [D, CH], BF16, kind="ExternalOutput").ap()
    logits = nc.dram_tensor("logits", [CH, 8], F32, kind="ExternalOutput").ap()

    with contextlib.ExitStack() as ctx:
        E = ctx.enter_context
        xacc = E(nc.sbuf_tensor("xacc", [128, 16 * D], F32))
        xnT_sb = E(nc.sbuf_tensor("xnT_sb", [128, 8 * CH], BF16))
        idb = E(nc.sbuf_tensor("idb", [128, 128], BF16))
        idf = E(nc.sbuf_tensor("idf", [128, 128], F32))
        onesb = E(nc.sbuf_tensor("onesb", [1, 128], BF16))
        onesf = E(nc.sbuf_tensor("onesf", [1, 128], F32))
        qsc = E(nc.sbuf_tensor("qsc", [128, 7 * 32], F32))
        qsi = E(nc.sbuf_tensor("qsi", [128, 32], I32))
        b2b = E(nc.sbuf_tensor("b2b", [1, D], BF16))
        rbf = E(nc.sbuf_tensor("rbf", [1, 8], F32))
        rw_sb = E(nc.sbuf_tensor("rw_sb", [128, 8 * 8], F32))
        sq = E(nc.sbuf_tensor("sq", [128, D], F32))
        st = E(nc.sbuf_tensor("st", [128, 8], F32))
        xn1b = E(nc.sbuf_tensor("xn1b", [128, D], BF16))
        xn1f = E(nc.sbuf_tensor("xn1f", [128, D], F32))
        xnf_t = E(nc.sbuf_tensor("xnf_t", [128, 8 * 128], F32))
        logit_sb = E(nc.sbuf_tensor("logit_sb", [128, 16 * 8], F32))
        xn1t_t = [E(nc.sbuf_tensor(f"xn1t_t{i}", [128, 1024], BF16)) for i in range(2)]
        sy = Sem(ctx, nc, "sy")
        sx0 = [Sem(ctx, nc, f"sx0{i}") for i in range(2)]
        sw1 = [Sem(ctx, nc, f"sw1{i}") for i in range(2)]
        sw2 = [Sem(ctx, nc, f"sw2{i}") for i in range(2)]
        sst = [Sem(ctx, nc, f"sst{i}") for i in range(2)]
        gp = Sem(ctx, nc, "gp")
        gm = Sem(ctx, nc, "gm")
        gld = Sem(ctx, nc, "gld")
        sm = Sem(ctx, nc, "sm")
        sv = Sem(ctx, nc, "sv")
        ss = Sem(ctx, nc, "ss")
        so = Sem(ctx, nc, "so")
        block = E(nc.Block())

        def _(g):
            g.dma_start(out=idb[:], in_=ident).then_inc(gp.h, 16)
            g.dma_start(out=idf[:], in_=ident).then_inc(gp.h, 16)
            g.dma_start(out=qsc[:, 0:32], in_=b1v).then_inc(gp.h, 16)
            g.dma_start(out=qsc[:, 32:64], in_=thv).then_inc(gp.h, 16)
            g.dma_start(out=qsc[:, 64:96], in_=wrv).then_inc(gp.h, 16)
            g.dma_start(out=qsc[:, 96:128], in_=wiv).then_inc(gp.h, 16)
            g.dma_start(out=b2b[0:1, :], in_=b2row).then_inc(gp.h, 16)
            g.dma_start(out=rbf[0:1, :], in_=rbrow).then_inc(gp.h, 16)
            g.dma_start(out=rw_sb[:].rearrange("p (a e) -> p a e", a=8),
                        in_=rw.rearrange("(a p) e -> p a e", p=128)).then_inc(gp.h, 16)
            g.dma_start(out=xnT_sb[:].rearrange("p (a t) -> p a t", a=8),
                        in_=xnTc.rearrange("(a p) t -> p a t", p=128)).then_inc(gp.h, 16)
            gp.n += 160
            g.memset(onesb[0:1, :], 1.0)
            g.memset(onesf[0:1, :], 1.0)
            gm.tick(g.engine_nop())
        block.gpsimd(_)
        consts = gp.n

        def _(v):
            v.wait_ge(gp.h, consts)
            v.wait_ge(gm.h, 1)
            emit_kround(v, qsc[:, 32:64], qsc[:, 128:160], sq[:, 0:32], qsi[:, 0:32],
                        scr_g=sq[:, 224:256], small=True)
            v.tensor_scalar(sq[:, 32:64], qsc[:, 32:64], PIH, None, ALU.add)
            v.drain()
            emit_kround(v, sq[:, 32:64], sq[:, 64:96], sq[:, 0:32], qsi[:, 0:32],
                        scr_g=sq[:, 224:256], small=True)
            sv.tick(v.engine_nop())
        block.vector(_)
        m1 = sv.n

        def _(s):
            s.wait_ge(sv.h, m1)
            s.activation(sq[:, 96:128], qsc[:, 128:160], AF.Sin)
            ss.tick(s.activation(sq[:, 32:64], sq[:, 64:96], AF.Sin))
        block.scalar(_)
        m2 = ss.n

        def _(v):
            v.wait_ge(ss.h, m2)
            v.tensor_tensor(sq[:, 0:32], sq[:, 32:64], qsc[:, 64:96], ALU.mult)
            v.tensor_tensor(sq[:, 128:160], sq[:, 96:128], qsc[:, 96:128], ALU.mult)
            v.drain()
            v.tensor_tensor(sq[:, 0:32], sq[:, 0:32], sq[:, 128:160], ALU.add)
            v.drain()
            v.tensor_tensor(sq[:, 0:32], sq[:, 0:32], qsc[:, 96:128], ALU.mult)
            v.drain()
            v.tensor_scalar(qsc[:, 160:192], sq[:, 0:32], 0.1, None, ALU.mult)
            v.scalar_tensor_tensor(qsc[:, 192:224], qsc[:, 0:32], 0.1, qsc[:, 128:160],
                                   ALU.mult, ALU.add)
            v.drain()
            sv.tick(v.engine_nop())
        block.vector(_)
        qconsts = sv.n

        with contextlib.ExitStack() as c2:
            E2 = c2.enter_context
            aoT_sb = E2(nc.sbuf_tensor("aoT_sb", [128, 8 * CH], BF16))
            wo_sb = E2(nc.sbuf_tensor("wo_sb", [128, 8 * D], BF16))
            x0t = [E2(nc.sbuf_tensor(f"x0t{i}", [128, D], F32)) for i in range(2)]
            pso = [E2(nc.psum_tensor(f"pso{i}", [128, 1024], F32)) for i in range(2)]

            def _(g):
                gld.tick(g.dma_start(out=aoT_sb[:].rearrange("p (a t) -> p a t", a=8),
                                     in_=aoTc.rearrange("(a p) t -> p a t", p=128)), 16)
                gld.tick(g.dma_start(out=wo_sb[:].rearrange("p (a n) -> p a n", a=8),
                                     in_=woutT.rearrange("(a p) n -> p a n", p=128)), 16)
            block.gpsimd(_)
            wo_ready = gld.n

            ld_mark = {}
            mm_mark = {}
            ev_mark = {}
            for mt in range(16):
                pp = mt % 2

                def _(s, mt=mt, pp=pp):
                    if mt >= 2:
                        s.wait_ge(sv.h, ev_mark[mt - 2])
                    ld_mark[mt] = sx0[pp].tick(s.dma_start(out=x0t[pp][:],
                                                           in_=x0[mt * 128:(mt + 1) * 128, :]), 16)
                block.sync(_)

                def _(t, mt=mt, pp=pp):
                    t.wait_ge(gld.h, wo_ready)
                    t.wait_ge(gm.h, 1)
                    if mt >= 2:
                        t.wait_ge(sv.h, ev_mark[mt - 2])
                    for nh in range(2):
                        for kt in range(8):
                            ins = t.matmul(pso[pp][:, nh * 512:(nh + 1) * 512],
                                           aoT_sb[:, kt * CH + mt * 128: kt * CH + (mt + 1) * 128],
                                           wo_sb[:, kt * D + nh * 512: kt * D + (nh + 1) * 512],
                                           start=(kt == 0), stop=(kt == 7))
                    mm_mark[mt] = sm.tick(ins)
                block.tensor(_)

                def _(v, mt=mt, pp=pp):
                    v.wait_ge(sm.h, mm_mark[mt])
                    v.wait_ge(sx0[pp].h, ld_mark[mt])
                    ev_mark[mt] = sv.tick(v.tensor_tensor(xacc[:, mt * D:(mt + 1) * D],
                                                          x0t[pp][:], pso[pp][:], ALU.add))
                block.vector(_)
            wout_v = sv.n

        yev = sv.n
        with contextlib.ExitStack() as cf:
            EF = cf.enter_context
            act_sb = EF(nc.sbuf_tensor("act_sb", [128, 32 * 512], BF16))
            w1b = [EF(nc.sbuf_tensor(f"w1b{i}", [128, 8 * 512], BF16)) for i in range(2)]
            h_sb = [EF(nc.sbuf_tensor(f"h_sb{i}", [128, 512], F32)) for i in range(2)]
            z_sb = [EF(nc.sbuf_tensor(f"z_sb{i}", [128, 512], F32)) for i in range(2)]
            zw_sb = [EF(nc.sbuf_tensor(f"zw_sb{i}", [128, 512], F32)) for i in range(2)]
            sz_sb = [EF(nc.sbuf_tensor(f"sz_sb{i}", [128, 512], F32)) for i in range(2)]
            kr_f = EF(nc.sbuf_tensor("kr_f", [128, 512], F32))
            kr_g = EF(nc.sbuf_tensor("kr_g", [128, 512], F32))
            kr_i = EF(nc.sbuf_tensor("kr_i", [128, 512], I32))
            q_sb = EF(nc.sbuf_tensor("q_sb", [128, 512], F32))
            w2t = [EF(nc.sbuf_tensor(f"w2t{i}", [128, D], BF16)) for i in range(2)]
            for g4 in range(4):
                actfree = sm.n
                with contextlib.ExitStack() as c2:
                    E2 = c2.enter_context
                    psh = [E2(nc.psum_tensor(f"psh{g4}{i}", [128, 512], F32)) for i in range(2)]
                    va = []
                    wld = {}
                    hm = []
                    for fb in range(8):
                        ppw = fb % 2

                        def _(g, g4=g4, fb=fb, ppw=ppw):
                            if fb >= 2:
                                g.wait_ge(sm.h, hm[(fb - 1) * 4 - 1])
                            else:
                                g.wait_ge(sv.h, wout_v)
                            wld[fb] = sw1[ppw].tick(g.dma_start(
                                out=w1b[ppw][:].rearrange("p (a m) -> p a m", a=8),
                                in_=W1[:, fb * 512:(fb + 1) * 512].rearrange("(a p) m -> p a m", p=128)), 16)
                        block.gpsimd(_)

                        for ftl in range(4):
                            ft = fb * 4 + ftl
                            li = fb * 4 + ftl
                            pp = li % 2

                            def _(t, g4=g4, fb=fb, ftl=ftl, pp=pp, ppw=ppw, li=li, yev=yev):
                                t.wait_ge(sw1[ppw].h, wld[fb])
                                t.wait_ge(sv.h, qconsts)
                                t.wait_ge(sv.h, wout_v)
                                if li >= 2:
                                    t.wait_ge(sv.h, va[li - 2])
                                else:
                                    t.wait_ge(sv.h, yev)
                                for kt in range(8):
                                    ins = t.matmul(psh[pp][:, :],
                                                   w1b[ppw][:, kt * 512 + ftl * 128: kt * 512 + (ftl + 1) * 128],
                                                   xnT_sb[:, kt * CH + g4 * 512: kt * CH + (g4 + 1) * 512],
                                                   start=(kt == 0), stop=(kt == 7))
                                hm.append(sm.tick(ins))
                            block.tensor(_)

                            def _(s, pp=pp, ft=ft, li=li):
                                s.wait_ge(sm.h, hm[li])
                                if li >= 2:
                                    s.wait_ge(sv.h, va[li - 2])
                                s.activation(h_sb[pp][:], psh[pp][:, :], AF.Identity,
                                             bias=qsc[:, ft:ft + 1])
                                ss.tick(s.activation(z_sb[pp][:], psh[pp][:, :], AF.Identity,
                                                     bias=qsc[:, 192 + ft:193 + ft], scale=0.1))
                            block.scalar(_)
                            zm = ss.n

                            def _(v, pp=pp, zm=zm):
                                v.wait_ge(ss.h, zm)
                                emit_kround(v, z_sb[pp][:], zw_sb[pp][:], kr_f[:], kr_i[:],
                                            scr_g=kr_g[:])
                                sv.tick(v.engine_nop())
                            block.vector(_)
                            km = sv.n

                            def _(s, pp=pp, km=km):
                                s.wait_ge(sv.h, km)
                                ss.tick(s.activation(sz_sb[pp][:], zw_sb[pp][:], AF.Sin))
                            block.scalar(_)
                            sm3 = ss.n

                            def _(v, pp=pp, ft=ft, sm3=sm3, actfree=actfree):
                                v.wait_ge(ss.h, sm3)
                                v.wait_ge(sm.h, actfree)
                                v.scalar_tensor_tensor(q_sb[:], sz_sb[pp][:], qsc[:, 160 + ft:161 + ft],
                                                       h_sb[pp][:], ALU.mult, ALU.add)
                                v.drain()
                                ins = v.tensor_scalar(
                                    act_sb[:, ft * 512:(ft + 1) * 512],
                                    q_sb[:], 0.0, None, ALU.max)
                                va.append(sv.tick(ins))
                            block.vector(_)

                act_ready = sv.n
                with contextlib.ExitStack() as c2:
                    E2 = c2.enter_context
                    yps = [E2(nc.psum_tensor(f"yps{g4}{i}", [128, 1024], F32)) for i in range(4)]
                    w2m = {}
                    mmk = {}
                    yev_now = yev
                    tg_prev_mm = sm.n
                    for kt in range(33):
                        if kt < 32:
                            ppw = kt % 2

                            def _(g, kt=kt, ppw=ppw, act_ready=act_ready, tg_prev_mm=tg_prev_mm):
                                if kt >= 2:
                                    g.wait_ge(sm.h, mmk[kt - 2])
                                else:
                                    g.wait_ge(sv.h, act_ready)
                                    g.wait_ge(sm.h, tg_prev_mm)
                                w2m[kt] = sw2[ppw].tick(g.dma_start(
                                    out=w2t[ppw][:], in_=W2[kt * 128:(kt + 1) * 128, :]), 16)
                            block.gpsimd(_)

                        def _(t, g4=g4, kt=kt, yev_now=yev_now, act_ready=act_ready):
                            ins = None
                            if kt < 32:
                                t.wait_ge(sw2[kt % 2].h, w2m[kt])
                                t.wait_ge(sv.h, act_ready)
                                if kt == 0:
                                    t.wait_ge(sv.h, yev_now)
                                for mt4 in range(4):
                                    for nh in range(2):
                                        ins = t.matmul(
                                            yps[mt4][:, nh * 512:(nh + 1) * 512],
                                            act_sb[:, kt * 512 + mt4 * 128: kt * 512 + (mt4 + 1) * 128],
                                            w2t[kt % 2][:, nh * 512:(nh + 1) * 512],
                                            start=(kt == 0), stop=False)
                            else:
                                for mt4 in range(4):
                                    for nh in range(2):
                                        ins = t.matmul(
                                            yps[mt4][:, nh * 512:(nh + 1) * 512],
                                            onesb[0:1, 0:128],
                                            b2b[0:1, nh * 512:(nh + 1) * 512],
                                            start=False, stop=True)
                            mmk[kt] = sm.tick(ins)
                        block.tensor(_)
                    ydone = sm.n

                    def _(v, g4=g4, ydone=ydone):
                        v.wait_ge(sm.h, ydone)
                        for mt4 in range(4):
                            mtg = g4 * 4 + mt4
                            ins = v.tensor_tensor(xacc[:, mtg * D:(mtg + 1) * D],
                                                  xacc[:, mtg * D:(mtg + 1) * D],
                                                  yps[mt4][:], ALU.add)
                        sv.tick(ins)
                    block.vector(_)
                    yev = sv.n

        ffn_v = sv.n
        with contextlib.ExitStack() as c2:
            E2 = c2.enter_context
            pstb = E2(nc.psum_tensor("pstb", [128, 1024], BF16))
            pstf = E2(nc.psum_tensor("pstf", [128, 1024], F32))
            rps = E2(nc.psum_tensor("rps", [128, 8], F32))
            tpf_mark = {}
            cpb_mark = {}
            cpf_mark = {}
            rmm_mark = {}
            rcp_mark = {}
            xnst_mark = {}
            for mt in range(16):
                def _(v, mt=mt):
                    v.wait_ge(sv.h, ffn_v)
                    if mt >= 1:
                        v.wait_ge(sm.h, tpf_mark[mt - 1])
                block.vector(_)
                ln_done = emit_ln(block, sq, st, sv, ss,
                                  xacc[:, mt * D:(mt + 1) * D], xn1b[:], xn1f[:])

                def _(t, mt=mt, ln_done=ln_done):
                    t.wait_ge(sv.h, ln_done)
                    t.wait_ge(gp.h, consts)
                    if mt >= 1:
                        t.wait_ge(so.h, cpf_mark[mt - 1])
                    for dt in range(8):
                        t.transpose(pstb[:, dt * 128:(dt + 1) * 128],
                                    xn1b[:, dt * 128:(dt + 1) * 128], idb[:])
                    for dt in range(8):
                        ins2 = t.transpose(pstf[:, dt * 128:(dt + 1) * 128],
                                           xn1f[:, dt * 128:(dt + 1) * 128], idf[:])
                    tpf_mark[mt] = sm.tick(ins2)
                block.tensor(_)

                def _(v, mt=mt):
                    v.wait_ge(sm.h, tpf_mark[mt])
                    if mt >= 1:
                        v.wait_ge(sm.h, rmm_mark[mt - 1])
                    if mt >= 2:
                        v.wait_ge(sst[mt % 2].h, xnst_mark[mt - 2])
                    cpb_mark[mt] = so.tick(v.tensor_copy(xn1t_t[mt % 2][:], pstb[:]))
                    cpf_mark[mt] = so.tick(v.tensor_copy(xnf_t[:], pstf[:]))
                block.vector(_)

                def _(s, mt=mt):
                    s.wait_ge(so.h, cpb_mark[mt])
                    xnst_mark[mt] = sst[mt % 2].tick(s.dma_start(
                        out=xn1T[:, mt * 128:(mt + 1) * 128].rearrange("(a p) c -> p a c", p=128),
                        in_=xn1t_t[mt % 2][:].rearrange("p (a c) -> p a c", a=8)), 16)
                block.sync(_)

                def _(t, mt=mt):
                    t.wait_ge(so.h, cpf_mark[mt])
                    if mt >= 1:
                        t.wait_ge(so.h, rcp_mark[mt - 1])
                    for kt in range(8):
                        t.matmul(rps[:, :], xnf_t[:, kt * 128:(kt + 1) * 128],
                                 rw_sb[:, kt * 8:(kt + 1) * 8],
                                 start=(kt == 0), stop=False)
                    ins = t.matmul(rps[:, :], onesf[0:1, 0:128], rbf[0:1, :],
                                   start=False, stop=True)
                    rmm_mark[mt] = sm.tick(ins)
                block.tensor(_)

                def _(v, mt=mt):
                    v.wait_ge(sm.h, rmm_mark[mt])
                    rcp_mark[mt] = so.tick(v.tensor_copy(logit_sb[:, mt * 8:(mt + 1) * 8], rps[:, :]))
                block.vector(_)

            def _(s):
                s.wait_ge(so.h, cpb_mark[15])
                s.wait_ge(so.h, rcp_mark[15])
                s.wait_ge(sst[0].h, sst[0].n)
                s.wait_ge(sst[1].h, sst[1].n)
                s.dma_start(out=x1c.rearrange("(a p) d -> p a d", p=128),
                            in_=xacc[:].rearrange("p (a d) -> p a d", a=16)).then_inc(sy.h, 16)
                s.dma_start(out=logits.rearrange("(a p) e -> p a e", p=128),
                            in_=logit_sb[:].rearrange("p (a e) -> p a e", a=16)).then_inc(sy.h, 16)
                sy.n += 32
                s.wait_ge(sy.h, sy.n)
            block.sync(_)
    return nc



def emit_hgrn_sub(block, nc, sems, bufs, s_i, want_p):
    sm, sv, ss = sems
    wi_sb = bufs["wi_sb"]; wf_sb = bufs["wf_sb"]; xe = bufs["xe_sb"]
    cvi = bufs["cvi"]; cvf = bufs["cvf"]
    iext = bufs["iext"]; fext = bufs["fext"]
    cvb_i = bufs["cvb_i"]; cvb_f = bufs["cvb_f"]
    fg = bufs["fg"]; sil = bufs["sil"]; isg = bufs["isg"]; zz = bufs["zz"]
    hout = bufs["hout"]; hcar = bufs["hcar"]; pcar = bufs["pcar"]
    prs = bufs["prs"]
    e0 = s_i * 512

    for dt in range(8):
        for gate, w_sb, ext in ((0, wi_sb, iext), (1, wf_sb, fext)):
            def _(t, dt=dt, w_sb=w_sb, e0=e0):
                sm_prev = bufs["raw_ev"]
                t.wait_ge(sv.h, sm_prev[0])
                for kt in range(8):
                    t.matmul(prs[:, 0:512], w_sb[:, kt * 1024 + dt * 128:kt * 1024 + (dt + 1) * 128],
                             xe[:, kt * 2051 + e0: kt * 2051 + e0 + 512],
                             start=(kt == 0), stop=(kt == 7))
                for kt in range(8):
                    ins = t.matmul(prs[:, 512:515], w_sb[:, kt * 1024 + dt * 128:kt * 1024 + (dt + 1) * 128],
                                   xe[:, kt * 2051 + e0 + 512: kt * 2051 + e0 + 515],
                                   start=(kt == 0), stop=(kt == 7))
                bufs["raw_mm"][0] = sm.tick(ins)
            block.tensor(_)

            def _(v, ext=ext):
                v.wait_ge(sm.h, bufs["raw_mm"][0])
                ins = v.tensor_copy(ext[:, 0:515], prs[:, 0:515])
                bufs["raw_ev"][0] = sv.tick(ins)
            block.vector(_)

        def _(v, dt=dt):
            v.wait_ge(sv.h, bufs["raw_ev"][0])
            v.tensor_scalar(cvb_i[:], iext[:, 0:512], cvi[:, dt * 4:dt * 4 + 1], None, ALU.mult)
            v.tensor_scalar(cvb_f[:], fext[:, 0:512], cvf[:, dt * 4:dt * 4 + 1], None, ALU.mult)
            for k in range(1, 4):
                v.drain()
                v.scalar_tensor_tensor(cvb_i[:], iext[:, k:k + 512], cvi[:, dt * 4 + k:dt * 4 + k + 1],
                                       cvb_i[:], ALU.mult, ALU.add)
                v.scalar_tensor_tensor(cvb_f[:], fext[:, k:k + 512], cvf[:, dt * 4 + k:dt * 4 + k + 1],
                                       cvb_f[:], ALU.mult, ALU.add)
            v.drain()
            sv.tick(v.engine_nop())
        block.vector(_)
        cv_done = sv.n

        def _(s, cv_done=cv_done):
            s.wait_ge(sv.h, cv_done)
            s.activation(fg[:], cvb_f[:], AF.Sigmoid)
            ss.tick(s.activation(sil[:], cvb_i[:], AF.Sigmoid))
        block.scalar(_)
        g_done = ss.n

        def _(v, dt=dt, g_done=g_done, want_p=want_p):
            v.wait_ge(ss.h, g_done)
            v.tensor_tensor(sil[:], sil[:], cvb_i[:], ALU.mult)
            v.drain()
            v.tensor_tensor(isg[:], sil[:], fg[:], ALU.mult)
            v.drain()
            v.tensor_tensor(isg[:], sil[:], isg[:], ALU.subtract)
            v.drain()
            v.tensor_tensor_scan(hout[:, dt * 512:(dt + 1) * 512], fg[:], isg[:],
                                 hcar[:, dt:dt + 1], ALU.mult, ALU.add)
            v.drain()
            v.tensor_copy(hcar[:, dt:dt + 1], hout[:, dt * 512 + 511: dt * 512 + 512])
            if want_p:
                v.tensor_tensor_scan(isg[:], fg[:], zz[:], pcar[:, dt:dt + 1], ALU.mult, ALU.add)
                v.drain()
                v.tensor_copy(pcar[:, dt:dt + 1], isg[:, 511:512])
            v.drain()
            sv.tick(v.engine_nop())
        block.vector(_)


def build_b4():
    nc = bass.Bass("TRN2", target_bir_lowering=False, debug=False)
    xe_d = nc.dram_tensor("xn1ext", [D, CH + 3], BF16, kind="ExternalInput").ap()
    wi_d = nc.dram_tensor("wi", [D, D], F32, kind="ExternalInput").ap()
    wf_d = nc.dram_tensor("wf", [D, D], F32, kind="ExternalInput").ap()
    cvi_d = nc.dram_tensor("cvi", [128, 32], F32, kind="ExternalInput").ap()
    cvf_d = nc.dram_tensor("cvf", [128, 32], F32, kind="ExternalInput").ap()
    car_d = nc.dram_tensor("carries", [128, 16], F32, kind="ExternalOutput").ap()

    with contextlib.ExitStack() as ctx:
        E = ctx.enter_context
        wi_sb = E(nc.sbuf_tensor("wi_sb", [128, 8 * D], BF16))
        wf_sb = E(nc.sbuf_tensor("wf_sb", [128, 8 * D], BF16))
        xe_sb = E(nc.sbuf_tensor("xe_sb", [128, 8 * (CH + 3)], BF16))
        cvi = E(nc.sbuf_tensor("cvi_sb", [128, 32], F32))
        cvf = E(nc.sbuf_tensor("cvf_sb", [128, 32], F32))
        iext = E(nc.sbuf_tensor("iext", [128, 520], F32))
        fext = E(nc.sbuf_tensor("fext", [128, 520], F32))
        cvb_i = E(nc.sbuf_tensor("cvb_i", [128, 512], F32))
        cvb_f = E(nc.sbuf_tensor("cvb_f", [128, 512], F32))
        fg = E(nc.sbuf_tensor("fg", [128, 512], F32))
        sil = E(nc.sbuf_tensor("sil", [128, 512], F32))
        isg = E(nc.sbuf_tensor("isg", [128, 512], F32))
        zz = E(nc.sbuf_tensor("zz", [128, 512], F32))
        hout = E(nc.sbuf_tensor("hout", [128, 8 * 512], F32))
        car = E(nc.sbuf_tensor("car", [128, 16], F32))
        prs = E(nc.psum_tensor("prs", [128, 520], F32))
        sy = Sem(ctx, nc, "sy")
        gm = Sem(ctx, nc, "gm")
        gp = Sem(ctx, nc, "gp")
        sm = Sem(ctx, nc, "sm")
        sv = Sem(ctx, nc, "sv")
        ss = Sem(ctx, nc, "ss")
        block = E(nc.Block())

        def _(g):
            g.dma_start(out=wi_sb[:].rearrange("p (a d) -> p a d", a=8),
                        in_=wi_d.rearrange("(a p) d -> p a d", p=128)).then_inc(gp.h, 16)
            g.dma_start(out=wf_sb[:].rearrange("p (a d) -> p a d", a=8),
                        in_=wf_d.rearrange("(a p) d -> p a d", p=128)).then_inc(gp.h, 16)
            g.dma_start(out=xe_sb[:].rearrange("p (a t) -> p a t", a=8),
                        in_=xe_d.rearrange("(a p) t -> p a t", p=128)).then_inc(gp.h, 16)
            g.dma_start(out=cvi[:], in_=cvi_d).then_inc(gp.h, 16)
            g.dma_start(out=cvf[:], in_=cvf_d).then_inc(gp.h, 16)
            gp.n += 80
            g.memset(zz[:], 0.0)
            g.memset(car[:, 0:8], 0.0)
            g.memset(car[:, 8:16], 1.0)
            gm.tick(g.engine_nop())
        block.gpsimd(_)
        ready = gp.n

        def _(v):
            v.wait_ge(gp.h, ready)
            v.wait_ge(gm.h, 1)
            sv.tick(v.engine_nop())
        block.vector(_)

        def _(t):
            t.wait_ge(gp.h, ready)
            t.wait_ge(gm.h, 1)
        block.tensor(_)

        bufs = dict(wi_sb=wi_sb, wf_sb=wf_sb, xe_sb=xe_sb, cvi=cvi, cvf=cvf,
                    iext=iext, fext=fext, cvb_i=cvb_i, cvb_f=cvb_f,
                    fg=fg, sil=sil, isg=isg, zz=zz, hout=hout,
                    hcar=car[:, 0:8], pcar=car[:, 8:16], prs=prs,
                    raw_mm=[sm.n], raw_ev=[sv.n])
        for s_i in range(4):
            emit_hgrn_sub(block, nc, (sm, sv, ss), bufs, s_i, want_p=True)

        def _(s):
            s.wait_ge(sv.h, sv.n)
            sy.tick(s.dma_start(out=car_d, in_=car[:]), 16)
            s.wait_ge(sy.h, sy.n)
        block.sync(_)
    return nc


def build_b5():
    nc = bass.Bass("TRN2", target_bir_lowering=False, debug=False)
    xe_d = nc.dram_tensor("xn1ext", [D, CH + 3], BF16, kind="ExternalInput").ap()
    wi_d = nc.dram_tensor("wi", [D, D], F32, kind="ExternalInput").ap()
    wf_d = nc.dram_tensor("wf", [D, D], F32, kind="ExternalInput").ap()
    wg_d = nc.dram_tensor("wg", [D, D], F32, kind="ExternalInput").ap()
    wo_d = nc.dram_tensor("wo", [D, D], F32, kind="ExternalInput").ap()
    cvi_d = nc.dram_tensor("cvi", [128, 32], F32, kind="ExternalInput").ap()
    cvf_d = nc.dram_tensor("cvf", [128, 32], F32, kind="ExternalInput").ap()
    gnw_d = nc.dram_tensor("gnw", [128, 8], F32, kind="ExternalInput").ap()
    cars_d = nc.dram_tensor("carsall", [128, 128], F32, kind="ExternalInput").ap()
    self_d = nc.dram_tensor("selfl", [128, 8], F32, kind="ExternalInput").ap()
    x1_d = nc.dram_tensor("x1c", [CH, D], F32, kind="ExternalInput").ap()
    x15_d = nc.dram_tensor("x15c", [CH, D], F32, kind="ExternalOutput").ap()

    with contextlib.ExitStack() as ctx:
        E = ctx.enter_context
        wi_sb = E(nc.sbuf_tensor("wi_sb", [128, 8 * D], BF16))
        wf_sb = E(nc.sbuf_tensor("wf_sb", [128, 8 * D], BF16))
        wg_sb = E(nc.sbuf_tensor("wg_sb", [128, 8 * D], BF16))
        wo_sb = E(nc.sbuf_tensor("wo_sb", [128, 8 * D], BF16))
        xe_sb = E(nc.sbuf_tensor("xe_sb", [128, 8 * (CH + 3)], BF16))
        cvi = E(nc.sbuf_tensor("cvi_sb", [128, 32], F32))
        cvf = E(nc.sbuf_tensor("cvf_sb", [128, 32], F32))
        gnw = E(nc.sbuf_tensor("gnw_sb", [128, 8], F32))
        cars = E(nc.sbuf_tensor("cars_sb", [128, 128], F32))
        sfl = E(nc.sbuf_tensor("sfl_sb", [128, 8], F32))
        onesc = E(nc.sbuf_tensor("onesc", [128, 1], F32))
        onesr = E(nc.sbuf_tensor("onesr", [1, 128], F32))
        iext = E(nc.sbuf_tensor("iext", [128, 520], F32))
        fext = E(nc.sbuf_tensor("fext", [128, 520], F32))
        cvb_i = E(nc.sbuf_tensor("cvb_i", [128, 512], F32))
        cvb_f = E(nc.sbuf_tensor("cvb_f", [128, 512], F32))
        fg = E(nc.sbuf_tensor("fg", [128, 512], F32))
        sil = E(nc.sbuf_tensor("sil", [128, 512], F32))
        isg = E(nc.sbuf_tensor("isg", [128, 512], F32))
        zz = E(nc.sbuf_tensor("zz", [128, 512], F32))
        hout = E(nc.sbuf_tensor("hout", [128, 8 * 512], F32))
        silug = E(nc.sbuf_tensor("silug", [128, 8 * 512], BF16))
        oT = E(nc.sbuf_tensor("oT", [128, 8 * 512], BF16))
        sqa = E(nc.sbuf_tensor("sqa", [128, 512], F32))
        sqt = E(nc.sbuf_tensor("sqt", [128, 512], F32))
        rrow = E(nc.sbuf_tensor("rrow", [1, 512], F32))
        rbc = E(nc.sbuf_tensor("rbc", [128, 512], F32))
        hin = E(nc.sbuf_tensor("hin", [128, 8], F32))
        tca = E(nc.sbuf_tensor("tca", [128, 8], F32))
        tcb = E(nc.sbuf_tensor("tcb", [128, 8], F32))
        x1t = E(nc.sbuf_tensor("x1t", [128, D], F32))
        xot = E(nc.sbuf_tensor("xot", [128, D], F32))
        car = E(nc.sbuf_tensor("car", [128, 16], F32))
        prs = E(nc.psum_tensor("prs", [128, 520], F32))
        gps = E(nc.psum_tensor("gps", [128, 512], F32))
        rsp = E(nc.psum_tensor("rsp", [1, 512], F32))
        bps = E(nc.psum_tensor("bps", [128, 512], F32))
        wops = E(nc.psum_tensor("wops", [128, 1024], F32))
        sy = Sem(ctx, nc, "sy")
        gm = Sem(ctx, nc, "gm")
        gp = Sem(ctx, nc, "gp")
        sm = Sem(ctx, nc, "sm")
        sv = Sem(ctx, nc, "sv")
        ss = Sem(ctx, nc, "ss")
        so = Sem(ctx, nc, "so")
        block = E(nc.Block())

        def _(g):
            for dst, src in ((wi_sb, wi_d), (wf_sb, wf_d), (wg_sb, wg_d), (wo_sb, wo_d)):
                g.dma_start(out=dst[:].rearrange("p (a d) -> p a d", a=8),
                            in_=src.rearrange("(a p) d -> p a d", p=128)).then_inc(gp.h, 16)
            g.dma_start(out=xe_sb[:].rearrange("p (a t) -> p a t", a=8),
                        in_=xe_d.rearrange("(a p) t -> p a t", p=128)).then_inc(gp.h, 16)
            g.dma_start(out=cvi[:], in_=cvi_d).then_inc(gp.h, 16)
            g.dma_start(out=cvf[:], in_=cvf_d).then_inc(gp.h, 16)
            g.dma_start(out=gnw[:], in_=gnw_d).then_inc(gp.h, 16)
            g.dma_start(out=cars[:], in_=cars_d).then_inc(gp.h, 16)
            g.dma_start(out=sfl[:], in_=self_d).then_inc(gp.h, 16)
            gp.n += 160
            g.memset(zz[:], 0.0)
            g.memset(onesc[:], 1.0)
            g.memset(onesr[0:1, :], 1.0)
            g.memset(hin[:], 0.0)
            gm.tick(g.engine_nop())
        block.gpsimd(_)
        ready = gp.n

        def _(v):
            v.wait_ge(gp.h, ready)
            v.wait_ge(gm.h, 1)
            for j in range(7):
                v.tensor_tensor(tca[:], cars[:, j * 16 + 8: j * 16 + 16], hin[:], ALU.mult)
                v.drain()
                v.tensor_tensor(tca[:], tca[:], cars[:, j * 16: j * 16 + 8], ALU.add)
                v.drain()
                v.tensor_tensor(tcb[:], tca[:], hin[:], ALU.subtract)
                v.drain()
                v.scalar_tensor_tensor(hin[:], tcb[:], sfl[:, j:j + 1], hin[:], ALU.mult, ALU.add)
                v.drain()
            v.tensor_copy(car[:, 0:8], hin[:])
            v.drain()
            sv.tick(v.engine_nop())
        block.vector(_)

        def _(t):
            t.wait_ge(gp.h, ready)
            t.wait_ge(gm.h, 1)
            t.wait_ge(sv.h, sv.n)
        block.tensor(_)

        bufs = dict(wi_sb=wi_sb, wf_sb=wf_sb, xe_sb=xe_sb, cvi=cvi, cvf=cvf,
                    iext=iext, fext=fext, cvb_i=cvb_i, cvb_f=cvb_f,
                    fg=fg, sil=sil, isg=isg, zz=zz, hout=hout,
                    hcar=car[:, 0:8], pcar=car[:, 8:16], prs=prs,
                    raw_mm=[sm.n], raw_ev=[sv.n])

        gsil = {}
        xo_mark = [sv.n]
        o_prev = [sv.n]
        for s_i in range(4):
            for dt in range(8):
                def _(t, s_i=s_i, dt=dt):
                    if (s_i, dt) != (0, 0):
                        t.wait_ge(ss.h, gsil[(s_i, dt - 1) if dt else (s_i - 1, 7)])
                    for kt in range(8):
                        ins = t.matmul(gps[:, :],
                                       wg_sb[:, kt * 1024 + dt * 128: kt * 1024 + (dt + 1) * 128],
                                       xe_sb[:, kt * (CH + 3) + 3 + s_i * 512:
                                             kt * (CH + 3) + 3 + (s_i + 1) * 512],
                                       start=(kt == 0), stop=(kt == 7))
                    sm.tick(ins)
                block.tensor(_)
                gm = sm.n

                def _(s, dt=dt, s_i=s_i, gm=gm):
                    s.wait_ge(sm.h, gm)
                    s.wait_ge(sv.h, o_prev[0])
                    gsil[(s_i, dt)] = ss.tick(s.activation(silug[:, dt * 512:(dt + 1) * 512],
                                                           gps[:, :], AF.Silu))
                block.scalar(_)

            emit_hgrn_sub(block, nc, (sm, sv, ss), bufs, s_i, want_p=False)
            scan_done = sv.n

            def _(v, scan_done=scan_done):
                v.wait_ge(sv.h, scan_done)
                v.tensor_tensor(sqa[:], hout[:, 0:512], hout[:, 0:512], ALU.mult)
                for dt in range(1, 8):
                    v.tensor_tensor(sqt[:], hout[:, dt * 512:(dt + 1) * 512],
                                    hout[:, dt * 512:(dt + 1) * 512], ALU.mult)
                    v.drain()
                    v.tensor_tensor(sqa[:], sqa[:], sqt[:], ALU.add)
                    v.drain()
                sv.tick(v.engine_nop())
            block.vector(_)
            sq_done = sv.n

            def _(t, sq_done=sq_done):
                t.wait_ge(sv.h, sq_done)
                ins = t.matmul(rsp[0:1, :], onesc[:, 0:1], sqa[:], start=True, stop=True)
                sm.tick(ins)
            block.tensor(_)
            rs_mm = sm.n

            def _(v, rs_mm=rs_mm):
                v.wait_ge(sm.h, rs_mm)
                v.tensor_scalar(rrow[0:1, :], rsp[0:1, :], INV_D, LN_EPS, ALU.mult, ALU.add)
                v.drain()
                sv.tick(v.engine_nop())
            block.vector(_)
            rr1 = sv.n

            def _(s, rr1=rr1):
                s.wait_ge(sv.h, rr1)
                s.activation(rrow[0:1, :], rrow[0:1, :], AF.Ln)
                s.drain()
                ss.tick(s.activation(rrow[0:1, :], rrow[0:1, :], AF.Exp, scale=-0.5))
            block.scalar(_)
            rr2 = ss.n

            def _(t, rr2=rr2):
                t.wait_ge(ss.h, rr2)
                ins = t.matmul(bps[:, :], onesr[0:1, :], rrow[0:1, :], start=True, stop=True)
                sm.tick(ins)
            block.tensor(_)
            bc_mm = sm.n

            def _(v, s_i=s_i, bc_mm=bc_mm):
                v.wait_ge(sm.h, bc_mm)
                v.wait_ge(ss.h, gsil[(s_i, 7)])
                v.tensor_copy(rbc[:], bps[:, :])
                v.drain()
                for dt in range(8):
                    v.tensor_tensor(sqt[:], hout[:, dt * 512:(dt + 1) * 512], rbc[:], ALU.mult)
                    v.drain()
                    v.scalar_tensor_tensor(oT[:, dt * 512:(dt + 1) * 512], sqt[:],
                                           gnw[:, dt:dt + 1], silug[:, dt * 512:(dt + 1) * 512],
                                           ALU.mult, ALU.mult)
                    v.drain()
                sv.tick(v.engine_nop())
            block.vector(_)
            o_done = sv.n
            o_prev[0] = o_done

            for mt in range(4):
                gt = s_i * 4 + mt

                def _(s, gt=gt):
                    if gt >= 1:
                        s.wait_ge(sv.h, xo_mark[0])
                    sy.tick(s.dma_start(out=x1t[:], in_=x1_d[gt * 128:(gt + 1) * 128, :]), 16)
                block.sync(_)
                x1_ld = sy.n

                def _(t, mt=mt, o_done=o_done, gt=gt):
                    t.wait_ge(sv.h, o_done)
                    if gt >= 1:
                        t.wait_ge(sv.h, xo_mark[0])
                    for nh in range(2):
                        for kt in range(8):
                            ins = t.matmul(wops[:, nh * 512:(nh + 1) * 512],
                                           oT[:, kt * 512 + mt * 128: kt * 512 + (mt + 1) * 128],
                                           wo_sb[:, kt * 1024 + nh * 512: kt * 1024 + (nh + 1) * 512],
                                           start=(kt == 0), stop=(kt == 7))
                    sm.tick(ins)
                block.tensor(_)
                wo_mm = sm.n

                def _(v, wo_mm=wo_mm, x1_ld=x1_ld):
                    v.wait_ge(sm.h, wo_mm)
                    v.wait_ge(sy.h, x1_ld)
                    xo_mark[0] = sv.tick(v.tensor_tensor(xot[:], x1t[:], wops[:], ALU.add))
                block.vector(_)

                def _(s, gt=gt):
                    s.wait_ge(sv.h, xo_mark[0])
                    so.tick(s.dma_start(out=x15_d[gt * 128:(gt + 1) * 128, :], in_=xot[:]), 16)
                block.sync(_)
                def _(v):
                    v.wait_ge(so.h, so.n)
                block.vector(_)

        def _(s):
            s.wait_ge(so.h, so.n)
        block.sync(_)
    return nc



def build_b6():
    nc = bass.Bass("TRN2", target_bir_lowering=False, debug=False)
    xn1T = nc.dram_tensor("xn1Tc", [D, CH], BF16, kind="ExternalInput").ap()
    x15 = nc.dram_tensor("x15c", [CH, D], F32, kind="ExternalInput").ap()
    logit_d = nc.dram_tensor("logits", [CH, 8], F32, kind="ExternalInput").ap()
    w1_d = nc.dram_tensor("w1", [EXP, D, 2 * DFF], F32, kind="ExternalInput").ap()
    b1_d = nc.dram_tensor("b1", [EXP, 128, 64], F32, kind="ExternalInput").ap()
    w2_d = nc.dram_tensor("w2", [EXP, DFF, D], F32, kind="ExternalInput").ap()
    b2_d = nc.dram_tensor("b2", [EXP, 1, D], F32, kind="ExternalInput").ap()
    x2_d = nc.dram_tensor("x2c", [CH, D], F32, kind="ExternalOutput").ap()

    with contextlib.ExitStack() as ctx:
        E = ctx.enter_context
        xacc = E(nc.sbuf_tensor("xacc", [128, 16 * D], F32))
        act_sb = E(nc.sbuf_tensor("act_sb", [128, 32 * 1024], BF16))
        xn_g = E(nc.sbuf_tensor("xn_g", [128, 8 * 1024], BF16))
        lg_sb = E(nc.sbuf_tensor("lg_sb", [128, 16 * 8], F32))
        msk = E(nc.sbuf_tensor("msk", [128, 16 * 8], F32))
        mxc = E(nc.sbuf_tensor("mxc", [128, 2], F32))
        b1_sb = E(nc.sbuf_tensor("b1_sb", [128, 64], F32))
        b2_sb = E(nc.sbuf_tensor("b2_sb", [1, D], BF16))
        onesb = E(nc.sbuf_tensor("onesb", [1, 128], BF16))
        sgt = [E(nc.sbuf_tensor(f"sgt{i}", [128, 512], BF16)) for i in range(2)]
        w1a = [E(nc.sbuf_tensor(f"w1a{i}", [128, 8 * 512], BF16)) for i in range(2)]
        w1g = [E(nc.sbuf_tensor(f"w1g{i}", [128, 8 * 512], BF16)) for i in range(2)]
        w2t = [E(nc.sbuf_tensor(f"w2t{i}", [128, D], BF16)) for i in range(2)]
        sy = Sem(ctx, nc, "sy")
        sw1 = [Sem(ctx, nc, f"sw1{i}") for i in range(2)]
        sw2 = [Sem(ctx, nc, f"sw2{i}") for i in range(2)]
        sbb = Sem(ctx, nc, "sbb")
        sxg = Sem(ctx, nc, "sxg")
        gp = Sem(ctx, nc, "gp")
        gm = Sem(ctx, nc, "gm")
        sm = Sem(ctx, nc, "sm")
        sv = Sem(ctx, nc, "sv")
        ss = Sem(ctx, nc, "ss")
        block = E(nc.Block())

        def _(g):
            g.dma_start(out=xacc[:].rearrange("p (a d) -> p a d", a=16),
                        in_=x15.rearrange("(a p) d -> p a d", p=128)).then_inc(gp.h, 16)
            g.dma_start(out=lg_sb[:].rearrange("p (a e) -> p a e", a=16),
                        in_=logit_d.rearrange("(a p) e -> p a e", p=128)).then_inc(gp.h, 16)
            gp.n += 32
            g.memset(onesb[0:1, :], 1.0)
            gm.tick(g.engine_nop())
        block.gpsimd(_)
        base_ready = gp.n

        def _(v):
            v.wait_ge(gp.h, base_ready)
            v.wait_ge(gm.h, 1)
            for mt in range(16):
                v.reduce_max(mxc[:, 0:1], lg_sb[:, mt * 8:(mt + 1) * 8], axis=AX.X)
                v.memset(mxc[:, 1:2], 1.0)
                v.drain()
                for e in range(EXP):
                    v.tensor_tensor(msk[:, mt * 8 + e: mt * 8 + e + 1],
                                    lg_sb[:, mt * 8 + e: mt * 8 + e + 1], mxc[:, 0:1], ALU.is_equal)
                    v.drain()
                    v.tensor_tensor(msk[:, mt * 8 + e: mt * 8 + e + 1],
                                    msk[:, mt * 8 + e: mt * 8 + e + 1], mxc[:, 1:2], ALU.mult)
                    v.drain()
                    v.tensor_tensor(mxc[:, 1:2], mxc[:, 1:2],
                                    msk[:, mt * 8 + e: mt * 8 + e + 1], ALU.subtract)
                    v.drain()
            sv.tick(v.engine_nop())
        block.vector(_)
        msk_done = sv.n

        yev = [sv.n]
        prevB_mm = [sm.n]
        for e in range(EXP):
            def _(g, e=e):
                if e > 0:
                    g.wait_ge(sv.h, yev[0])
                sbb.tick(g.dma_start(out=b1_sb[:], in_=b1_d[e, :, :]), 16)
                sbb.tick(g.dma_start(out=b2_sb[0:1, :], in_=b2_d[e, 0:1, :]), 16)
            block.gpsimd(_)
            be_ready = sbb.n

            for g_i in range(2):
                def _(g, e=e, g_i=g_i):
                    g.wait_ge(sm.h, sm.n)
                    sxg.tick(g.dma_start(
                        out=xn_g[:].rearrange("p (a t) -> p a t", a=8),
                        in_=xn1T[:, g_i * 1024:(g_i + 1) * 1024]
                        .rearrange("(a p) t -> p a t", p=128)), 16)
                block.gpsimd(_)
                xn_ready = sxg.n

                with contextlib.ExitStack() as c2:
                    E2 = c2.enter_context
                    psa = [E2(nc.psum_tensor(f"psa{e}{g_i}{i}", [128, 512], F32)) for i in range(2)]
                    psg = [E2(nc.psum_tensor(f"psg{e}{g_i}{i}", [128, 512], F32)) for i in range(2)]
                    va = []
                    wld = {}
                    hm = []
                    for fb in range(8):
                        ppw = fb % 2

                        def _(g, e=e, fb=fb, ppw=ppw):
                            if fb >= 2:
                                g.wait_ge(sm.h, hm[(fb - 1) * 8 - 1])
                            sw1[ppw].tick(g.dma_start(
                                out=w1a[ppw][:].rearrange("p (a m) -> p a m", a=8),
                                in_=w1_d[e, :, fb * 512:(fb + 1) * 512]
                                .rearrange("(a p) m -> p a m", p=128)), 16)
                            wld[fb] = sw1[ppw].tick(g.dma_start(
                                out=w1g[ppw][:].rearrange("p (a m) -> p a m", a=8),
                                in_=w1_d[e, :, DFF + fb * 512: DFF + (fb + 1) * 512]
                                .rearrange("(a p) m -> p a m", p=128)), 16)
                        block.gpsimd(_)

                        for ftl in range(4):
                            ft = fb * 4 + ftl
                            for nch in range(2):
                                li = fb * 8 + ftl * 2 + nch
                                pp = li % 2

                                def _(t, fb=fb, ftl=ftl, pp=pp, ppw=ppw, li=li,
                                      be_ready=be_ready, xn_ready=xn_ready, nch=nch):
                                    t.wait_ge(sw1[ppw].h, wld[fb])
                                    t.wait_ge(sbb.h, be_ready)
                                    t.wait_ge(sxg.h, xn_ready)
                                    t.wait_ge(gm.h, 1)
                                    t.wait_ge(sv.h, msk_done)
                                    if li >= 2:
                                        t.wait_ge(sv.h, va[li - 2])
                                    else:
                                        t.wait_ge(sv.h, yev[0])
                                    rhs0 = nch * 512
                                    for kt in range(8):
                                        t.matmul(psg[pp][:, :],
                                                 w1g[ppw][:, kt * 512 + ftl * 128: kt * 512 + (ftl + 1) * 128],
                                                 xn_g[:, kt * 1024 + rhs0: kt * 1024 + rhs0 + 512],
                                                 start=(kt == 0), stop=(kt == 7))
                                    for kt in range(8):
                                        ins = t.matmul(psa[pp][:, :],
                                                       w1a[ppw][:, kt * 512 + ftl * 128: kt * 512 + (ftl + 1) * 128],
                                                       xn_g[:, kt * 1024 + rhs0: kt * 1024 + rhs0 + 512],
                                                       start=(kt == 0), stop=(kt == 7))
                                    hm.append(sm.tick(ins))
                                block.tensor(_)

                                def _(s, pp=pp, ft=ft, li=li):
                                    s.wait_ge(sm.h, hm[li])
                                    if li >= 2:
                                        s.wait_ge(sv.h, va[li - 2])
                                    ss.tick(s.activation(sgt[pp][:], psg[pp][:, :], AF.Sigmoid,
                                                         bias=b1_sb[:, 32 + ft: 33 + ft]))
                                block.scalar(_)
                                sgm = ss.n

                                def _(v, pp=pp, ft=ft, nch=nch, sgm=sgm):
                                    v.wait_ge(ss.h, sgm)
                                    v.wait_ge(sm.h, prevB_mm[0])
                                    ins = v.scalar_tensor_tensor(
                                        act_sb[:, ft * 1024 + nch * 512: ft * 1024 + (nch + 1) * 512],
                                        psa[pp][:, :], b1_sb[:, ft:ft + 1], sgt[pp][:],
                                        ALU.add, ALU.mult)
                                    va.append(sv.tick(ins))
                                block.vector(_)

                act_ready = sv.n

                with contextlib.ExitStack() as c2:
                    E2 = c2.enter_context
                    yps = [E2(nc.psum_tensor(f"yps{e}{g_i}{i}", [128, 1024], F32)) for i in range(4)]
                    for tg in range(2):
                        w2m = {}
                        mmk = {}
                        yev_now = yev[0]
                        tg_prev_mm = sm.n
                        for kt in range(33):
                            if kt < 32:
                                ppw = kt % 2

                                def _(g, e=e, kt=kt, ppw=ppw, tg_prev_mm=tg_prev_mm):
                                    if kt >= 2:
                                        g.wait_ge(sm.h, mmk[kt - 2])
                                    else:
                                        g.wait_ge(sm.h, tg_prev_mm)
                                    w2m[kt] = sw2[ppw].tick(g.dma_start(
                                        out=w2t[ppw][:], in_=w2_d[e, kt * 128:(kt + 1) * 128, :]), 16)
                                block.gpsimd(_)

                            def _(t, tg=tg, kt=kt, yev_now=yev_now, act_ready=act_ready):
                                ins = None
                                if kt < 32:
                                    t.wait_ge(sw2[kt % 2].h, w2m[kt])
                                    t.wait_ge(sv.h, act_ready)
                                    if kt == 0:
                                        t.wait_ge(sv.h, yev_now)
                                    for mt4 in range(4):
                                        for nh in range(2):
                                            ins = t.matmul(
                                                yps[mt4][:, nh * 512:(nh + 1) * 512],
                                                act_sb[:, kt * 1024 + tg * 512 + mt4 * 128:
                                                       kt * 1024 + tg * 512 + (mt4 + 1) * 128],
                                                w2t[kt % 2][:, nh * 512:(nh + 1) * 512],
                                                start=(kt == 0), stop=False)
                                else:
                                    for mt4 in range(4):
                                        for nh in range(2):
                                            ins = t.matmul(
                                                yps[mt4][:, nh * 512:(nh + 1) * 512],
                                                onesb[0:1, 0:128],
                                                b2_sb[0:1, nh * 512:(nh + 1) * 512],
                                                start=False, stop=True)
                                mmk[kt] = sm.tick(ins)
                            block.tensor(_)
                        ydone = sm.n

                        def _(v, e=e, g_i=g_i, tg=tg, ydone=ydone):
                            v.wait_ge(sm.h, ydone)
                            for mt4 in range(4):
                                gt = g_i * 8 + tg * 4 + mt4
                                ins = v.scalar_tensor_tensor(
                                    xacc[:, gt * D:(gt + 1) * D], yps[mt4][:],
                                    msk[:, gt * 8 + e: gt * 8 + e + 1],
                                    xacc[:, gt * D:(gt + 1) * D], ALU.mult, ALU.add)
                            sv.tick(ins)
                        block.vector(_)
                        yev[0] = sv.n
                prevB_mm[0] = sm.n

        def _(s):
            s.wait_ge(sv.h, sv.n)
            sy.tick(s.dma_start(out=x2_d.rearrange("(a p) d -> p a d", p=128),
                                in_=xacc[:].rearrange("p (a d) -> p a d", a=16)), 16)
            s.wait_ge(sy.h, sy.n)
        block.sync(_)
    return nc



def build_one():
    nc = bass.Bass("TRN2", target_bir_lowering=False, debug=False)
    xb = nc.dram_tensor("xb", [TSEQ, D], F32, kind="ExternalInput").ap()
    wqT = nc.dram_tensor("wqT", [D, D], F32, kind="ExternalInput").ap()
    wkT = nc.dram_tensor("wkT", [D, D], F32, kind="ExternalInput").ap()
    wvT = nc.dram_tensor("wvT", [D, D], F32, kind="ExternalInput").ap()
    woutT = nc.dram_tensor("woutT", [D, D], F32, kind="ExternalInput").ap()
    W1 = nc.dram_tensor("W1", [D, DFF], F32, kind="ExternalInput").ap()
    W2 = nc.dram_tensor("W2", [DFF, D], F32, kind="ExternalInput").ap()
    b1v = nc.dram_tensor("b1v", [128, 32], F32, kind="ExternalInput").ap()
    thv = nc.dram_tensor("thv", [128, 32], F32, kind="ExternalInput").ap()
    wrv = nc.dram_tensor("wrv", [128, 32], F32, kind="ExternalInput").ap()
    wiv = nc.dram_tensor("wiv", [128, 32], F32, kind="ExternalInput").ap()
    b2row = nc.dram_tensor("b2row", [1, D], F32, kind="ExternalInput").ap()
    rw = nc.dram_tensor("rw", [D, 8], F32, kind="ExternalInput").ap()
    rbrow = nc.dram_tensor("rbrow", [1, 8], F32, kind="ExternalInput").ap()
    wi_d = nc.dram_tensor("wi", [D, D], F32, kind="ExternalInput").ap()
    wf_d = nc.dram_tensor("wf", [D, D], F32, kind="ExternalInput").ap()
    wg_d = nc.dram_tensor("wg", [D, D], F32, kind="ExternalInput").ap()
    wo_d = nc.dram_tensor("wo", [D, D], F32, kind="ExternalInput").ap()
    cvi_d = nc.dram_tensor("cvi", [128, 32], F32, kind="ExternalInput").ap()
    cvf_d = nc.dram_tensor("cvf", [128, 32], F32, kind="ExternalInput").ap()
    gnw_d = nc.dram_tensor("gnw", [128, 8], F32, kind="ExternalInput").ap()
    w1_d = nc.dram_tensor("w1m", [EXP, D, 2 * DFF], F32, kind="ExternalInput").ap()
    b1_d = nc.dram_tensor("b1m", [EXP, 128, 64], F32, kind="ExternalInput").ap()
    w2_d = nc.dram_tensor("w2m", [EXP, DFF, D], F32, kind="ExternalInput").ap()
    b2_d = nc.dram_tensor("b2m", [EXP, 1, D], F32, kind="ExternalInput").ap()
    tri = nc.dram_tensor("tri", [128, 4 * 512], F32, kind="ExternalInput").ap()
    ident = nc.dram_tensor("ident", [128, 128], F32, kind="ExternalInput").ap()
    owno = nc.dram_tensor("owno", [1, 2], I32, kind="ExternalInput").ap()
    selb = nc.dram_tensor("selb", [16, 8 * 128], F32, kind="ExternalInput").ap()
    x2_d = nc.dram_tensor("x2c", [CH, D], F32, kind="ExternalOutput").ap()

    d_den = nc.dram_tensor("d_den", [16, TSEQ], F32).ap()
    d_xnT = nc.dram_tensor("d_xnT", [D, TSEQ], BF16).ap()
    d_x1 = nc.dram_tensor("d_x1", [TSEQ, D], F32).ap()
    d_aoT = nc.dram_tensor("d_aoT", [D, TSEQ], BF16).ap()
    d_xn1T = nc.dram_tensor("d_xn1T", [D, TSEQ], BF16).ap()
    d_lg = nc.dram_tensor("d_lg", [TSEQ, 8], F32).ap()
    d_h = nc.dram_tensor("d_h", [D, TSEQ], F32).ap()
    d_oxn = nc.dram_tensor("d_oxn", [D, CH], BF16).ap()
    d_oh = nc.dram_tensor("d_oh", [D, CH], F32).ap()
    d_ox1 = nc.dram_tensor("d_ox1", [CH, D], F32).ap()
    d_olg = nc.dram_tensor("d_olg", [CH, 8], F32).ap()

    with contextlib.ExitStack() as ctx:
        E = ctx.enter_context
        idb = E(nc.sbuf_tensor("idb", [128, 128], BF16))
        idf = E(nc.sbuf_tensor("idf", [128, 128], F32))
        onesb = E(nc.sbuf_tensor("onesb", [1, 128], BF16))
        onesf = E(nc.sbuf_tensor("onesf", [1, 128], F32))
        onesc = E(nc.sbuf_tensor("onesc", [128, 1], F32))
        m10 = E(nc.sbuf_tensor("m10", [128, 1], F32))
        b256 = E(nc.sbuf_tensor("b256", [128, 1], F32))
        tri_sb = E(nc.sbuf_tensor("tri_sb", [128, 4 * 512], BF16))
        qsc = E(nc.sbuf_tensor("qsc", [128, 7 * 32], F32))
        qsi = E(nc.sbuf_tensor("qsi", [128, 32], I32))
        b2b = E(nc.sbuf_tensor("b2b", [1, D], BF16))
        rbf = E(nc.sbuf_tensor("rbf", [1, 8], F32))
        rw_sb = E(nc.sbuf_tensor("rw_sb", [128, 8 * 8], F32))
        cvi = E(nc.sbuf_tensor("cvi_sb", [128, 32], F32))
        cvf = E(nc.sbuf_tensor("cvf_sb", [128, 32], F32))
        gnw = E(nc.sbuf_tensor("gnw_sb", [128, 8], F32))
        oo_sb = E(nc.sbuf_tensor("oo_sb", [1, 2], I32))
        sq = E(nc.sbuf_tensor("sq", [128, D], F32))
        st = E(nc.sbuf_tensor("st", [128, 8], F32))
        zz = E(nc.sbuf_tensor("zz", [128, 512], F32))

        sy = Sem(ctx, nc, "sy")
        syp = [Sem(ctx, nc, f"syp{i}") for i in range(2)]
        sst = [Sem(ctx, nc, f"sst{i}") for i in range(2)]
        gp = Sem(ctx, nc, "gp")
        gm = Sem(ctx, nc, "gm")
        gld = Sem(ctx, nc, "gld")
        gdy = Sem(ctx, nc, "gdy")
        sw1 = [Sem(ctx, nc, f"sw1{i}") for i in range(2)]
        sw2 = [Sem(ctx, nc, f"sw2{i}") for i in range(2)]
        sxin = [Sem(ctx, nc, f"sxin{i}") for i in range(2)]
        sbb = Sem(ctx, nc, "sbb")
        slg = Sem(ctx, nc, "slg")
        sdn = Sem(ctx, nc, "sdn")
        sm = Sem(ctx, nc, "sm")
        smQ = Sem(ctx, nc, "smQ")
        smS = Sem(ctx, nc, "smS")
        smA = Sem(ctx, nc, "smA")
        smT = Sem(ctx, nc, "smT")
        ss = Sem(ctx, nc, "ss")
        sv = Sem(ctx, nc, "sv")
        svm = Sem(ctx, nc, "svm")
        svn = Sem(ctx, nc, "svn")
        so = Sem(ctx, nc, "so")
        ALLSEMS = [sy, syp[0], syp[1], sst[0], sst[1], gp, gm, gld, gdy,
                   sw1[0], sw1[1], sw2[0], sw2[1], sxin[0], sxin[1], sbb, slg, sdn,
                   sm, smQ, smS, smA, smT, ss, sv, svm, svn, so]
        block = E(nc.Block())

        def barrier():
            vals = [(s.h, s.n) for s in ALLSEMS]

            def _w(e):
                for h, v in vals:
                    if v:
                        e.wait_ge(h, v)
            block.tensor(_w)
            block.vector(_w)
            block.scalar(_w)
            block.gpsimd(_w)
            block.sync(_w)

        own_reg = {}

        def _(g):
            g.dma_start(out=idb[:], in_=ident).then_inc(gp.h, 16)
            g.dma_start(out=idf[:], in_=ident).then_inc(gp.h, 16)
            g.dma_start(out=tri_sb[:], in_=tri).then_inc(gp.h, 16)
            g.dma_start(out=qsc[:, 0:32], in_=b1v).then_inc(gp.h, 16)
            g.dma_start(out=qsc[:, 32:64], in_=thv).then_inc(gp.h, 16)
            g.dma_start(out=qsc[:, 64:96], in_=wrv).then_inc(gp.h, 16)
            g.dma_start(out=qsc[:, 96:128], in_=wiv).then_inc(gp.h, 16)
            g.dma_start(out=b2b[0:1, :], in_=b2row).then_inc(gp.h, 16)
            g.dma_start(out=rbf[0:1, :], in_=rbrow).then_inc(gp.h, 16)
            g.dma_start(out=rw_sb[:].rearrange("p (a e) -> p a e", a=8),
                        in_=rw.rearrange("(a p) e -> p a e", p=128)).then_inc(gp.h, 16)
            g.dma_start(out=cvi[:], in_=cvi_d).then_inc(gp.h, 16)
            g.dma_start(out=cvf[:], in_=cvf_d).then_inc(gp.h, 16)
            g.dma_start(out=gnw[:], in_=gnw_d).then_inc(gp.h, 16)
            g.dma_start(out=oo_sb[:], in_=owno).then_inc(gp.h, 16)
            gp.n += 14 * 16
            g.memset(onesb[0:1, :], 1.0)
            g.memset(onesf[0:1, :], 1.0)
            g.memset(onesc[:], 1.0)
            g.memset(m10[:], -10.0)
            g.memset(b256[:], 256.5)
            g.memset(zz[:], 0.0)
            gm.tick(g.engine_nop())
            g.wait_ge(gp.h, gp.n)
            own_reg["tok"] = g.value_load(oo_sb[0:1, 0:1])
            own_reg["tile"] = g.value_load(oo_sb[0:1, 1:2])
        block.gpsimd(_)
        consts = gp.n

        def _(v):
            v.wait_ge(gp.h, consts)
            v.wait_ge(gm.h, 1)
            emit_kround(v, qsc[:, 32:64], qsc[:, 128:160], sq[:, 0:32], qsi[:, 0:32],
                        scr_g=sq[:, 224:256], small=True)
            v.tensor_scalar(sq[:, 32:64], qsc[:, 32:64], PIH, None, ALU.add)
            v.drain()
            emit_kround(v, sq[:, 32:64], sq[:, 64:96], sq[:, 0:32], qsi[:, 0:32],
                        scr_g=sq[:, 224:256], small=True)
            sv.tick(v.engine_nop())
        block.vector(_)
        m1 = sv.n

        def _(s):
            s.wait_ge(sv.h, m1)
            s.activation(sq[:, 96:128], qsc[:, 128:160], AF.Sin)
            ss.tick(s.activation(sq[:, 32:64], sq[:, 64:96], AF.Sin))
        block.scalar(_)
        m2 = ss.n

        def _(v):
            v.wait_ge(ss.h, m2)
            v.tensor_tensor(sq[:, 0:32], sq[:, 32:64], qsc[:, 64:96], ALU.mult)
            v.tensor_tensor(sq[:, 128:160], sq[:, 96:128], qsc[:, 96:128], ALU.mult)
            v.drain()
            v.tensor_tensor(sq[:, 0:32], sq[:, 0:32], sq[:, 128:160], ALU.add)
            v.drain()
            v.tensor_tensor(sq[:, 0:32], sq[:, 0:32], qsc[:, 96:128], ALU.mult)
            v.drain()
            v.tensor_scalar(qsc[:, 160:192], sq[:, 0:32], 0.1, None, ALU.mult)
            v.scalar_tensor_tensor(qsc[:, 192:224], qsc[:, 0:32], 0.1, qsc[:, 128:160],
                                   ALU.mult, ALU.add)
            v.drain()
            sv.tick(v.engine_nop())
        block.vector(_)

        barrier()

        with contextlib.ExitStack() as c2:
            E2 = c2.enter_context
            xt = [E2(nc.sbuf_tensor(f"p1xt{i}", [128, D], F32)) for i in range(2)]
            xn = [E2(nc.sbuf_tensor(f"p1xn{i}", [128, D], BF16)) for i in range(2)]
            xnt_t = [E2(nc.sbuf_tensor(f"p1tt{i}", [128, 1024], BF16)) for i in range(2)]
            pst = [E2(nc.psum_tensor(f"p1ps{i}", [128, 1024], BF16)) for i in range(2)]
            tp_mark = {}
            st_mark = {}
            cp_mark = {}
            for tt in range(TSEQ // 128):
                pp = tt % 2

                def _(s_, tt=tt, pp=pp):
                    if tt >= 2:
                        s_.wait_ge(sm.h, tp_mark[tt - 2])
                    syp[pp].tick(s_.dma_start(out=xt[pp][:], in_=xb[tt * 128:(tt + 1) * 128, :]), 16)
                block.sync(_)
                ld = syp[pp].n

                def _(v, pp=pp, ld=ld):
                    v.wait_ge(syp[pp].h, ld)
                block.vector(_)
                ln_done = emit_ln(block, sq, st, sv, ss, xt[pp][:], xn[pp][:])

                def _(t, tt=tt, pp=pp, ln_done=ln_done):
                    t.wait_ge(sv.h, ln_done)
                    if tt >= 2:
                        t.wait_ge(so.h, cp_mark[tt - 2])
                    for dt in range(8):
                        ins = t.transpose(pst[pp][:, dt * 128:(dt + 1) * 128],
                                          xn[pp][:, dt * 128:(dt + 1) * 128], idb[:])
                    tp_mark[tt] = sm.tick(ins)
                block.tensor(_)

                def _(v, tt=tt, pp=pp):
                    v.wait_ge(sm.h, tp_mark[tt])
                    if tt >= 2:
                        v.wait_ge(sst[pp].h, st_mark[tt - 2])
                    cp_mark[tt] = so.tick(v.tensor_copy(xnt_t[pp][:], pst[pp][:]))
                block.vector(_)

                def _(s_, tt=tt, pp=pp):
                    s_.wait_ge(so.h, cp_mark[tt])
                    st_mark[tt] = sst[pp].tick(s_.dma_start(
                        out=d_xnT[:, tt * 128:(tt + 1) * 128].rearrange("(a p) c -> p a c", p=128),
                        in_=xnt_t[pp][:].rearrange("p (a c) -> p a c", a=8)), 16)
                block.sync(_)
            barrier()

        with contextlib.ExitStack() as c2:
            E2 = c2.enter_context
            w2r = E2(nc.sbuf_tensor("p2w2r", [128, 32 * 1024], BF16))
            xng = E2(nc.sbuf_tensor("p2xng", [128, 8 * 1024], BF16))
            act_sb = E2(nc.sbuf_tensor("p2act", [128, 32 * 1024], BF16))
            w1b = [E2(nc.sbuf_tensor(f"p2w1{i}", [128, 8 * 512], BF16)) for i in range(2)]
            z_sb = E2(nc.sbuf_tensor("p2z", [128, 512], F32))
            zw_sb = [E2(nc.sbuf_tensor(f"p2zw{i}", [128, 512], F32)) for i in range(2)]
            sz_sb = [E2(nc.sbuf_tensor(f"p2sz{i}", [128, 512], F32)) for i in range(2)]
            kri = E2(nc.sbuf_tensor("p2kri", [128, 512], I32))
            kr_g = E2(nc.sbuf_tensor("p2krg", [128, 512], F32))
            x0t = [E2(nc.sbuf_tensor(f"p2x0{i}", [128, D], F32)) for i in range(2)]
            x1t = [E2(nc.sbuf_tensor(f"p2x1{i}", [128, D], F32)) for i in range(2)]

            def _(g):
                for q4 in range(4):
                    g.dma_start(
                        out=w2r[:].rearrange("p (a n) -> p a n", a=32)[:, q4 * 8:(q4 + 1) * 8, :],
                        in_=W2[q4 * 1024:(q4 + 1) * 1024, :].rearrange("(a p) n -> p a n", p=128)
                    ).then_inc(gld.h, 16)
                gld.n += 64
            block.gpsimd(_)
            w2_ready = gld.n

            yev = sv.n
            x1a = {}
            x1s = {}
            x0m = {}
            prevA = [None]
            for g8 in range(8):
                def _(g, g8=g8, pA=prevA[0]):
                    if pA is None:
                        g.wait_ge(sm.h, sm.n)
                    else:
                        g.wait_ge(sm.h, pA)
                    gdy.tick(g.dma_start(
                        out=xng[:].rearrange("p (a t) -> p a t", a=8),
                        in_=d_xnT[:, g8 * 1024:(g8 + 1) * 1024].rearrange("(a p) t -> p a t", p=128)), 16)
                block.gpsimd(_)
                xn_ready = gdy.n

                with contextlib.ExitStack() as c3:
                    E3 = c3.enter_context
                    psh = [E3(nc.psum_tensor(f"p2ph{g8}{i}", [128, 512], F32)) for i in range(2)]
                    va = []
                    wld = {}
                    hm = []
                    v7m = {}
                    sm3_m = {}
                    actfree = sm.n
                    for fb in range(8):
                        ppw = fb % 2

                        def _(g, fb=fb, ppw=ppw, yev=yev):
                            if fb >= 2:
                                g.wait_ge(sm.h, hm[(fb - 1) * 8 - 1])
                            else:
                                g.wait_ge(sv.h, yev)
                            wld[fb] = sw1[ppw].tick(g.dma_start(
                                out=w1b[ppw][:].rearrange("p (a m) -> p a m", a=8),
                                in_=W1[:, fb * 512:(fb + 1) * 512].rearrange("(a p) m -> p a m", p=128)), 16)
                        block.gpsimd(_)

                        for ftl in range(4):
                            ft = fb * 4 + ftl
                            for nch in range(2):
                                li = fb * 8 + ftl * 2 + nch
                                pp = li % 2

                                def _(t, fb=fb, ftl=ftl, nch=nch, pp=pp, ppw=ppw, li=li,
                                      xn_ready=xn_ready, yev=yev):
                                    t.wait_ge(sw1[ppw].h, wld[fb])
                                    t.wait_ge(gdy.h, xn_ready)
                                    if li >= 2:
                                        t.wait_ge(sv.h, va[li - 2])
                                    else:
                                        t.wait_ge(sv.h, yev)
                                    for kt in range(8):
                                        ins = t.matmul(
                                            psh[pp][:, :],
                                            w1b[ppw][:, kt * 512 + ftl * 128: kt * 512 + (ftl + 1) * 128],
                                            xng[:, kt * 1024 + nch * 512: kt * 1024 + (nch + 1) * 512],
                                            start=(kt == 0), stop=(kt == 7))
                                    hm.append(sm.tick(ins))
                                block.tensor(_)

                                def _(v, pp=pp, ft=ft, li=li):
                                    v.wait_ge(sm.h, hm[li])
                                    if li >= 2:
                                        v.wait_ge(sv.h, va[li - 2])
                                    v.drain()
                                    v.tensor_scalar(z_sb[:], psh[pp][:, :], 0.1,
                                                    qsc[:, 192 + ft:193 + ft], ALU.mult, ALU.add)
                                    v.drain()
                                    v.tensor_scalar(kri[:], z_sb[:], INV2PI,
                                                    b256[:, 0:1], ALU.mult, ALU.add)
                                    v.drain()
                                    v.tensor_scalar(kr_g[:], kri[:], 256, None, ALU.subtract)
                                    v.drain()
                                    v.scalar_tensor_tensor(zw_sb[pp][:], kr_g[:], -TWO_PI,
                                                           z_sb[:], ALU.mult, ALU.add)
                                    v.drain()
                                    v.tensor_scalar(kr_g[:], zw_sb[pp][:], -PI_F, None, ALU.is_lt)
                                    v.drain()
                                    ins = v.scalar_tensor_tensor(zw_sb[pp][:], kr_g[:], TWO_PI,
                                                                 zw_sb[pp][:], ALU.mult, ALU.add)
                                    v7m[li] = sv.tick(ins)
                                block.vector(_)

                                if li >= 1:
                                    def _(v, pq=(li - 1) % 2, fq=(li - 1) // 8 * 4 + ((li - 1) % 8) // 2,
                                          sp=sm3_m[li - 1], actfree=actfree, lj=li - 1):
                                        v.wait_ge(ss.h, sp)
                                        v.wait_ge(sm.h, actfree)
                                        v.drain()
                                        v.scalar_tensor_tensor(kr_g[:], sz_sb[pq][:],
                                                               qsc[:, 160 + fq:161 + fq],
                                                               psh[pq][:, :], ALU.mult, ALU.add)
                                        v.drain()
                                        ins = v.tensor_scalar(
                                            act_sb[:, fq * 1024 + (lj % 2) * 512:
                                                   fq * 1024 + (lj % 2) * 512 + 512],
                                            kr_g[:], qsc[:, fq:fq + 1], 0.0, ALU.add, ALU.max)
                                        va.append(sv.tick(ins))
                                    block.vector(_)

                                def _(s_, pp=pp, li=li):
                                    s_.wait_ge(sv.h, v7m[li])
                                    if li >= 2:
                                        s_.wait_ge(sv.h, va[li - 2])
                                    sm3_m[li] = ss.tick(s_.activation(sz_sb[pp][:], zw_sb[pp][:], AF.Sin))
                                block.scalar(_)

                    def _(v):
                        li = 63
                        pq = li % 2
                        fq = li // 8 * 4 + (li % 8) // 2
                        v.wait_ge(ss.h, sm3_m[li])
                        v.drain()
                        v.scalar_tensor_tensor(kr_g[:], sz_sb[pq][:],
                                               qsc[:, 160 + fq:161 + fq],
                                               psh[pq][:, :], ALU.mult, ALU.add)
                        v.drain()
                        ins = v.tensor_scalar(
                            act_sb[:, fq * 1024 + (li % 2) * 512: fq * 1024 + (li % 2) * 512 + 512],
                            kr_g[:], qsc[:, fq:fq + 1], 0.0, ALU.add, ALU.max)
                        va.append(sv.tick(ins))
                    block.vector(_)
                    prevA[0] = hm[-1]
                act_ready = sv.n
                with contextlib.ExitStack() as c3:
                    E3 = c3.enter_context
                    yps = [E3(nc.psum_tensor(f"p2y{g8}{i}", [128, 1024], F32)) for i in range(4)]
                    for tg in range(2):
                        yev_now = yev

                        def _(t, tg=tg, yev_now=yev_now, act_ready=act_ready, w2_ready=w2_ready):
                            t.wait_ge(gld.h, w2_ready)
                            t.wait_ge(sv.h, act_ready)
                            t.wait_ge(sv.h, yev_now)
                            ins = None
                            for kt in range(32):
                                for mt4 in range(4):
                                    for nh in range(2):
                                        ins = t.matmul(
                                            yps[mt4][:, nh * 512:(nh + 1) * 512],
                                            act_sb[:, kt * 1024 + tg * 512 + mt4 * 128:
                                                   kt * 1024 + tg * 512 + (mt4 + 1) * 128],
                                            w2r[:, kt * 1024 + nh * 512: kt * 1024 + (nh + 1) * 512],
                                            start=(kt == 0), stop=False)
                            for mt4 in range(4):
                                for nh in range(2):
                                    ins = t.matmul(yps[mt4][:, nh * 512:(nh + 1) * 512],
                                                   onesb[0:1, 0:128],
                                                   b2b[0:1, nh * 512:(nh + 1) * 512],
                                                   start=False, stop=True)
                            sm.tick(ins)
                        block.tensor(_)
                        ydone = sm.n

                        for mt4 in range(4):
                            tt = g8 * 8 + tg * 4 + mt4
                            tp = tt % 2

                            def _(s_, tt=tt, tp=tp, a=x1a.get(tt - 2)):
                                if a is not None:
                                    s_.wait_ge(sv.h, a)
                                x0m[tt] = syp[tp].tick(s_.dma_start(
                                    out=x0t[tp][:], in_=xb[tt * 128:(tt + 1) * 128, :]), 16)
                            block.sync(_)

                            def _(v, mt4=mt4, tt=tt, tp=tp, ydone=ydone,
                                  st_=x1s.get(tt - 2)):
                                v.wait_ge(sm.h, ydone)
                                v.wait_ge(syp[tp].h, x0m[tt])
                                if st_ is not None:
                                    v.wait_ge(sst[tp].h, st_)
                                x1a[tt] = sv.tick(v.tensor_tensor(
                                    x1t[tp][:], x0t[tp][:], yps[mt4][:], ALU.add))
                            block.vector(_)

                            def _(s_, tt=tt, tp=tp):
                                s_.wait_ge(sv.h, x1a[tt])
                                x1s[tt] = sst[tp].tick(s_.dma_start(
                                    out=d_x1[tt * 128:(tt + 1) * 128, :], in_=x1t[tp][:]), 16)
                            block.sync(_)
                        yev = sv.n
            barrier()

        with contextlib.ExitStack() as c2:
            E2 = c2.enter_context
            wq_sb = E2(nc.sbuf_tensor("p3wq", [128, 8 * 128], BF16))
            wk_sb = E2(nc.sbuf_tensor("p3wk", [128, 8 * 128], BF16))
            wv_sb = E2(nc.sbuf_tensor("p3wv", [128, 8 * 128], BF16))
            QT = E2(nc.sbuf_tensor("p3QT", [128, TSEQ], BF16))
            KP = [E2(nc.sbuf_tensor(f"p3KP{i}", [128, TSEQ], BF16)) for i in range(2)]
            VO = E2(nc.sbuf_tensor("p3VO", [128, 64 * 130], BF16))
            VT_t = E2(nc.sbuf_tensor("p3VTt", [128, 512], BF16))
            xin = [E2(nc.sbuf_tensor(f"p3xi{i}", [128, 8 * 512], BF16)) for i in range(2)]
            expw = [E2(nc.sbuf_tensor(f"p3ex{i}", [128, 1024], BF16)) for i in range(2)]
            den_sb = E2(nc.sbuf_tensor("p3dn", [65, 1024], F32))
            ao_h = [E2(nc.sbuf_tensor(f"p3ao{h}", [64, TSEQ], BF16)) for h in range(2)]
            qkps = E2(nc.psum_tensor("p3qk", [128, 1024], F32))
            psq = qkps[:, 0:512]
            psk = qkps[:, 512:1024]
            psv = qkps[:, 0:512]
            pstp = E2(nc.psum_tensor("p3pt", [128, 512], BF16))
            scpw = [E2(nc.psum_tensor(f"p3sw{i}", [128, 1024], F32)) for i in range(2)]
            avp2 = E2(nc.psum_tensor("p3av", [128, 512], F32))

            def _(v):
                v.memset(KP[0][64:128, :], 0.0)
                v.memset(KP[1][0:64, :], 0.0)
                sv.tick(v.engine_nop())
            block.vector(_)
            kp_init = sv.n

            NQC = TSEQ // 512
            p3ev = {}
            p3dn = {}
            for hp in range(8):
                def _(g, hp=hp):
                    g.wait_ge(smA.h, smA.n)
                    for dst, src in ((wq_sb, wqT), (wk_sb, wkT), (wv_sb, wvT)):
                        ins = g.dma_start(
                            out=dst[:].rearrange("p (a m) -> p a m", a=8),
                            in_=src[:, hp * 128:(hp + 1) * 128].rearrange("(a p) m -> p a m", p=128))
                        ins.then_inc(gld.h, 16)
                    gld.n += 48
                    g.memset(VO[:], 1.0)
                    gm.tick(g.engine_nop())
                block.gpsimd(_)
                w_ready = gld.n
                vo_init = gm.n

                ld_mark = {}
                vev_mark = {}
                q_mark = {}
                k_mark = {}
                tp_mark = {}
                vo_mark = {}
                mmv_mark = {}
                mmqk_mark = {}
                attnS_prev = smS.n
                attnA_prev = smA.n
                svn_prev = svn.n
                smQ_prev = smQ.n
                ao_prev = sy.n
                for nn in range(NQC):
                    pp = nn % 2

                    def _(s_, nn=nn, pp=pp):
                        if nn >= 2:
                            s_.wait_ge(smQ.h, mmv_mark[nn - 2])
                        else:
                            s_.wait_ge(smQ.h, smQ_prev)
                        ld_mark[nn] = sxin[pp].tick(s_.dma_start(
                            out=xin[pp][:].rearrange("p (a t) -> p a t", a=8),
                            in_=d_xnT[:, nn * 512:(nn + 1) * 512].rearrange("(a p) t -> p a t", p=128)), 16)
                    block.sync(_)

                    def _(t, nn=nn, pp=pp, w_ready=w_ready, vo_init=vo_init):
                        t.wait_ge(sxin[pp].h, ld_mark[nn])
                        t.wait_ge(gld.h, w_ready)
                        t.wait_ge(gm.h, vo_init)
                        if nn >= 1:
                            t.wait_ge(sv.h, vev_mark[nn - 1])
                            t.wait_ge(ss.h, k_mark[nn - 1])
                        for kt in range(8):
                            t.matmul(psq, wq_sb[:, kt * 128:(kt + 1) * 128],
                                     xin[pp][:, kt * 512:(kt + 1) * 512],
                                     start=(kt == 0), stop=(kt == 7))
                        for kt in range(8):
                            ins = t.matmul(psk, wk_sb[:, kt * 128:(kt + 1) * 128],
                                           xin[pp][:, kt * 512:(kt + 1) * 512],
                                           start=(kt == 0), stop=(kt == 7))
                        mmqk_mark[nn] = smQ.tick(ins)
                    block.tensor(_)

                    def _(v, nn=nn):
                        if nn == 0:
                            v.wait_ge(smS.h, attnS_prev)
                        v.wait_ge(smQ.h, mmqk_mark[nn])
                        q_mark[nn] = sv.tick(v.tensor_scalar(QT[:, nn * 512:(nn + 1) * 512],
                                                             psq, 0.125, None, ALU.mult))
                    block.vector(_)

                    def _(t, nn=nn, pp=pp):
                        t.wait_ge(sv.h, q_mark[nn])
                        for kt in range(8):
                            ins = t.matmul(psv, wv_sb[:, kt * 128:(kt + 1) * 128],
                                           xin[pp][:, kt * 512:(kt + 1) * 512],
                                           start=(kt == 0), stop=(kt == 7))
                        mmv_mark[nn] = smQ.tick(ins)
                    block.tensor(_)

                    def _(v, nn=nn):
                        v.wait_ge(smQ.h, mmv_mark[nn])
                        if nn >= 1:
                            v.wait_ge(smT.h, tp_mark[nn - 1])
                        vev_mark[nn] = sv.tick(v.tensor_copy(VT_t[:], psv))
                    block.vector(_)

                    def _(s_, nn=nn, kp_init=kp_init):
                        if nn == 0:
                            s_.wait_ge(smS.h, attnS_prev)
                            s_.wait_ge(sv.h, kp_init)
                        s_.wait_ge(smQ.h, mmqk_mark[nn])
                        s_.activation(KP[0][0:64, nn * 512:(nn + 1) * 512],
                                      qkps[0:64, 512:1024], AF.Copy)
                        k_mark[nn] = ss.tick(s_.activation(KP[1][64:128, nn * 512:(nn + 1) * 512],
                                                           qkps[64:128, 512:1024], AF.Copy))
                    block.scalar(_)

                    def _(t, nn=nn):
                        t.wait_ge(sv.h, vev_mark[nn])
                        if nn >= 1:
                            t.wait_ge(sv.h, vo_mark[nn - 1])
                        else:
                            t.wait_ge(svn.h, svn_prev)
                        for s4 in range(4):
                            ins = t.transpose(pstp[:, s4 * 128:(s4 + 1) * 128],
                                              VT_t[:, s4 * 128:(s4 + 1) * 128], idb[:])
                        tp_mark[nn] = smT.tick(ins)
                    block.tensor(_)

                    def _(v, nn=nn):
                        v.wait_ge(smT.h, tp_mark[nn])
                        if nn == 0:
                            v.wait_ge(smA.h, attnA_prev)
                        for s4 in range(4):
                            kk = nn * 4 + s4
                            ins = v.tensor_copy(
                                VO[:, kk * 130: kk * 130 + 130].rearrange("p (h c) -> p h c", h=2)[:, :, 0:64],
                                pstp[:, s4 * 128:(s4 + 1) * 128].rearrange("p (h c) -> p h c", h=2))
                        vo_mark[nn] = sv.tick(ins)
                    block.vector(_)

                qkv_v = sv.n
                qkv_s = ss.n

                for h in range(2):
                    for qc in range(NQC):
                        gidx = hp * 32 + h * 16 + qc
                        par = gidx % 2
                        nkk = 4 * qc + 4
                        sc0 = smS.n
                        av0 = smA.n
                        ex0 = ss.n
                        cl0 = svm.n

                        npk = nkk // 2

                        def _(t, h=h, qc=qc, npk=npk, sc0=sc0, ex0=ex0, cl0=cl0,
                              par=par, evp=p3ev.get(gidx - 1), dnp=p3dn.get(gidx - 2),
                              qkv_v=qkv_v, qkv_s=qkv_s):
                            t.wait_ge(sv.h, qkv_v)
                            t.wait_ge(ss.h, qkv_s)
                            if dnp is not None:
                                t.wait_ge((slg if par == 0 else sdn).h, dnp)
                            for kp in range(npk + 1):
                                if kp < npk:
                                    if kp >= 2:
                                        t.wait_ge(ss.h, ex0 + kp - 1)
                                    for hf in range(2):
                                        kk = 2 * kp + hf
                                        ins = t.matmul(scpw[kp % 2][:, hf * 512:(hf + 1) * 512],
                                                       KP[h][:, kk * 128:(kk + 1) * 128],
                                                       QT[:, qc * 512:(qc + 1) * 512],
                                                       start=True, stop=True)
                                    smS.tick(ins)
                                if kp >= 1:
                                    ap = kp - 1
                                    if ap == 0 and evp is not None:
                                        t.wait_ge(svn.h, evp)
                                    t.wait_ge(svm.h, cl0 + ap + 1)
                                    for hf in range(2):
                                        ak = 2 * ap + hf
                                        ins = t.matmul(
                                            avp2[0:65, :],
                                            VO[:, ak * 130 + 65 * h: ak * 130 + 65 * h + 65],
                                            expw[ap % 2][:, hf * 512:(hf + 1) * 512],
                                            start=(ak == 0), stop=(ak == nkk - 1))
                                    smA.tick(ins)
                        block.tensor(_)

                        def _(s_, npk=npk, sc0=sc0, av0=av0):
                            for kp in range(npk):
                                s_.wait_ge(smS.h, sc0 + kp + 1)
                                if kp >= 2:
                                    s_.wait_ge(smA.h, av0 + kp - 1)
                                ss.tick(s_.activation(expw[kp % 2][:], scpw[kp % 2][:, :],
                                                      AF.Exp, bias=m10[:, 0:1]))
                        block.scalar(_)

                        def _(v, qc=qc, npk=npk, ex0=ex0):
                            for kp in range(npk):
                                v.wait_ge(ss.h, ex0 + kp + 1)
                                v.tensor_scalar(expw[kp % 2][:], expw[kp % 2][:],
                                                1.0, E20, ALU.min, ALU.max)
                                for hf in range(2):
                                    vd = 2 * kp + hf - 4 * qc
                                    if vd >= 0:
                                        v.drain()
                                        v.tensor_tensor(expw[kp % 2][:, hf * 512:(hf + 1) * 512],
                                                        expw[kp % 2][:, hf * 512:(hf + 1) * 512],
                                                        tri_sb[:, vd * 512:(vd + 1) * 512], ALU.mult)
                                svm.tick(v.engine_nop())
                        block.vector(_)

                        av_all = av0 + npk

                        def _(v, h=h, qc=qc, par=par, av_all=av_all, ao_prev=ao_prev):
                            v.wait_ge(smA.h, av_all)
                            v.wait_ge(sy.h, ao_prev)
                            base = qc * 512
                            v.tensor_copy(ao_h[h][0:64, base:base + 512], avp2[0:64, :])
                            ins = v.tensor_copy(den_sb[64:65, par * 512:(par + 1) * 512],
                                                avp2[64:65, :])
                            svn.tick(ins)
                        block.vector(_)
                        p3ev[gidx] = svn.n

                        dsem = slg if par == 0 else sdn

                        def _(s_, hp=hp, h=h, qc=qc, par=par, dsem=dsem, ev=p3ev[gidx]):
                            s_.wait_ge(svn.h, ev)
                            dsem.tick(s_.dma_start(
                                out=d_den[2 * hp + h: 2 * hp + h + 1, qc * 512:(qc + 1) * 512],
                                in_=den_sb[64:65, par * 512:(par + 1) * 512]), 16)
                        block.sync(_)
                        p3dn[gidx] = dsem.n

                def _(s_, hp=hp):
                    s_.wait_ge(svn.h, svn.n)
                    s_.dma_start(out=d_aoT[hp * 128:hp * 128 + 64, :], in_=ao_h[0][0:64, :]).then_inc(sy.h, 16)
                    s_.dma_start(out=d_aoT[hp * 128 + 64:hp * 128 + 128, :], in_=ao_h[1][0:64, :]).then_inc(sy.h, 16)
                    sy.n += 32
                block.sync(_)
            barrier()

        with contextlib.ExitStack() as c2:
            E2 = c2.enter_context
            aoT_sb = E2(nc.sbuf_tensor("p4ao", [128, 8 * CH], BF16))
            wo_sb = E2(nc.sbuf_tensor("p4wo", [128, 8 * D], BF16))
            x1p = [E2(nc.sbuf_tensor(f"p4x1{i}", [128, D], F32)) for i in range(2)]
            x1f = E2(nc.sbuf_tensor("p4x1f", [128, D], F32))
            xn1b = E2(nc.sbuf_tensor("p4nb", [128, D], BF16))
            xn1f = E2(nc.sbuf_tensor("p4nf", [128, D], F32))
            xnf_t = E2(nc.sbuf_tensor("p4nt", [128, 8 * 128], F32))
            logit_sb = E2(nc.sbuf_tensor("p4lg", [128, 16 * 8], F32))
            xn1t_t = [E2(nc.sbuf_tensor(f"p4tt{i}", [128, 1024], BF16)) for i in range(2)]
            selb_sb = E2(nc.sbuf_tensor("p4sel", [16, 8 * 128], F32))
            den16 = E2(nc.sbuf_tensor("p4dn", [16, CH], F32))
            rc16 = E2(nc.sbuf_tensor("p4rc", [16, CH], F32))
            pso = [E2(nc.psum_tensor(f"p4po{i}", [128, 1024], F32)) for i in range(2)]
            pstb = E2(nc.psum_tensor("p4pb", [128, 1024], BF16))
            pstf = E2(nc.psum_tensor("p4pf", [128, 1024], F32))
            p4ms = E2(nc.psum_tensor("p4ms", [128, 512], F32))
            
            def _(g):
                gld.tick(g.dma_start(out=wo_sb[:].rearrange("p (a n) -> p a n", a=8),
                                     in_=woutT.rearrange("(a p) n -> p a n", p=128)), 16)
                gld.tick(g.dma_start(out=selb_sb[:], in_=selb), 16)
            block.gpsimd(_)
            wo_ready = gld.n

            ld_mark = {}
            mm_mark = {}
            ev_mark = {}
            tpf_mark = {}
            cpb_mark = {}
            cpf_mark = {}
            rmm_mark = {}
            rcp_mark = {}
            st_mark = {}
            x1st_mark = {}
            lgst_mark = {}
            nm_mark = {}
            for ct in range(4):
                def _(g, ct=ct):
                    g.wait_ge(sm.h, sm.n)
                    g.dma_start(out=den16[:, :],
                                in_=d_den[:, ct * CH:(ct + 1) * CH]).then_inc(gdy.h, 16)
                    gdy.n += 16
                    gdy.tick(g.dma_start(
                        out=aoT_sb[:].rearrange("p (a t) -> p a t", a=8),
                        in_=d_aoT[:, ct * CH:(ct + 1) * CH].rearrange("(a p) t -> p a t", p=128)), 16)
                block.gpsimd(_)
                ao_ready = gdy.n

                def _(v, ao_ready=ao_ready):
                    v.wait_ge(gdy.h, ao_ready)
                    v.reciprocal(rc16[:, :], den16[:, :])
                    v.drain()
                    sv.tick(v.engine_nop())
                block.vector(_)
                rcm = sv.n

                for ia in range(32):
                    a, seg = ia // 4, ia % 4

                    def _(t, a=a, seg=seg, ia=ia, ct=ct, rcm=rcm, wo_ready=wo_ready):
                        t.wait_ge(gld.h, wo_ready)
                        t.wait_ge(sv.h, rcm)
                        if ia == 0 and ct >= 1:
                            t.wait_ge(so.h, rcp_mark[ct * 16 - 1])
                        if ia >= 1:
                            t.wait_ge(so.h, nm_mark[(ct, ia - 1)])
                        ins = t.matmul(p4ms[:, :], selb_sb[:, a * 128:(a + 1) * 128],
                                       rc16[:, seg * 512:(seg + 1) * 512],
                                       start=True, stop=True)
                        sm.tick(ins)
                    block.tensor(_)
                    bcm = sm.n

                    def _(v, a=a, seg=seg, ia=ia, ct=ct, bcm=bcm, ao_ready=ao_ready):
                        v.wait_ge(sm.h, bcm)
                        v.wait_ge(gdy.h, ao_ready)
                        cbase = a * CH + seg * 512
                        ins = v.tensor_tensor(aoT_sb[:, cbase:cbase + 512],
                                              aoT_sb[:, cbase:cbase + 512],
                                              p4ms[:, :], ALU.mult)
                        nm_mark[(ct, ia)] = so.tick(ins)
                    block.vector(_)
                nm_last = nm_mark[(ct, 31)]

                for mt in range(16):
                    tt = ct * 16 + mt
                    pp = tt % 2

                    def _(s_, tt=tt, pp=pp):
                        if tt >= 2:
                            s_.wait_ge(sv.h, ev_mark[tt - 2])
                        ld_mark[tt] = syp[pp].tick(s_.dma_start(
                            out=x1p[pp][:], in_=d_x1[tt * 128:(tt + 1) * 128, :]), 16)
                    block.sync(_)

                    def _(t, tt=tt, mt=mt, pp=pp, ao_ready=ao_ready, wo_ready=wo_ready,
                          nm_last=nm_last):
                        t.wait_ge(gld.h, wo_ready)
                        t.wait_ge(gdy.h, ao_ready)
                        t.wait_ge(gm.h, 1)
                        if mt == 0:
                            t.wait_ge(so.h, nm_last)
                        if tt >= 2:
                            t.wait_ge(sv.h, ev_mark[tt - 2])
                        for nh in range(2):
                            for kt in range(8):
                                ins = t.matmul(pso[pp][:, nh * 512:(nh + 1) * 512],
                                               aoT_sb[:, kt * CH + mt * 128: kt * CH + (mt + 1) * 128],
                                               wo_sb[:, kt * D + nh * 512: kt * D + (nh + 1) * 512],
                                               start=(kt == 0), stop=(kt == 7))
                        mm_mark[tt] = sm.tick(ins)
                    block.tensor(_)

                    def _(v, tt=tt, pp=pp):
                        v.wait_ge(sm.h, mm_mark[tt])
                        v.wait_ge(syp[pp].h, ld_mark[tt])
                        if tt >= 1:
                            v.wait_ge(sm.h, tpf_mark[tt - 1])
                            v.wait_ge((sy if (tt - 1) % 2 == 0 else sdn).h,
                                      x1st_mark[tt - 1])
                        ev_mark[tt] = sv.tick(v.tensor_tensor(x1f[:], x1p[pp][:], pso[pp][:], ALU.add))
                    block.vector(_)

                    def _(s_, tt=tt):
                        s_.wait_ge(sv.h, ev_mark[tt])
                        (sy if tt % 2 == 0 else sdn).tick(
                            s_.dma_start(out=d_x1[tt * 128:(tt + 1) * 128, :], in_=x1f[:]), 16)
                    block.sync(_)
                    x1st_mark[tt] = (sy if tt % 2 == 0 else sdn).n

                    def _(v, tt=tt):
                        v.wait_ge(sv.h, ev_mark[tt])
                    block.vector(_)
                    ln_done = emit_ln(block, sq, st, sv, ss, x1f[:], xn1b[:], xn1f[:])

                    def _(t, tt=tt, ln_done=ln_done):
                        t.wait_ge(sv.h, ln_done)
                        if tt >= 1:
                            t.wait_ge(so.h, cpf_mark[tt - 1])
                        for dt in range(8):
                            t.transpose(pstb[:, dt * 128:(dt + 1) * 128],
                                        xn1b[:, dt * 128:(dt + 1) * 128], idb[:])
                        for dt in range(8):
                            ins2 = t.transpose(pstf[:, dt * 128:(dt + 1) * 128],
                                               xn1f[:, dt * 128:(dt + 1) * 128], idf[:])
                        tpf_mark[tt] = sm.tick(ins2)
                    block.tensor(_)

                    def _(v, tt=tt, ct=ct, mt=mt):
                        v.wait_ge(sm.h, tpf_mark[tt])
                        if tt >= 1:
                            v.wait_ge(sm.h, rmm_mark[tt - 1])
                        if tt >= 2:
                            v.wait_ge(sst[tt % 2].h, st_mark[tt - 2])
                        if mt == 0 and ct >= 1:
                            v.wait_ge(slg.h, lgst_mark[ct - 1])
                        cpb_mark[tt] = so.tick(v.tensor_copy(xn1t_t[tt % 2][:], pstb[:]))
                        cpf_mark[tt] = so.tick(v.tensor_copy(xnf_t[:], pstf[:]))
                    block.vector(_)

                    def _(s_, tt=tt):
                        s_.wait_ge(so.h, cpb_mark[tt])
                        st_mark[tt] = sst[tt % 2].tick(s_.dma_start(
                            out=d_xn1T[:, tt * 128:(tt + 1) * 128].rearrange("(a p) c -> p a c", p=128),
                            in_=xn1t_t[tt % 2][:].rearrange("p (a c) -> p a c", a=8)), 16)
                    block.sync(_)

                    def _(t, tt=tt):
                        t.wait_ge(so.h, cpf_mark[tt])
                        if tt >= 1:
                            t.wait_ge(so.h, rcp_mark[tt - 1])
                        for kt in range(8):
                            t.matmul(p4ms[:, 0:8], xnf_t[:, kt * 128:(kt + 1) * 128],
                                     rw_sb[:, kt * 8:(kt + 1) * 8],
                                     start=(kt == 0), stop=False)
                        ins = t.matmul(p4ms[:, 0:8], onesf[0:1, 0:128], rbf[0:1, :],
                                       start=False, stop=True)
                        rmm_mark[tt] = sm.tick(ins)
                    block.tensor(_)

                    def _(v, tt=tt, mt=mt):
                        v.wait_ge(sm.h, rmm_mark[tt])
                        rcp_mark[tt] = so.tick(v.tensor_copy(logit_sb[:, mt * 8:(mt + 1) * 8], p4ms[:, 0:8]))
                    block.vector(_)

                def _(s_, ct=ct):
                    s_.wait_ge(so.h, rcp_mark[ct * 16 + 15])
                    slg.tick(s_.dma_start(
                        out=d_lg[ct * CH:(ct + 1) * CH, :].rearrange("(a p) e -> p a e", p=128),
                        in_=logit_sb[:].rearrange("p (a e) -> p a e", a=16)), 16)
                block.sync(_)
                lgst_mark[ct] = slg.n
            barrier()

        with contextlib.ExitStack() as c2:
            E2 = c2.enter_context
            wi_sb = E2(nc.sbuf_tensor("p5wi", [128, 8 * D], BF16))
            wf_sb = E2(nc.sbuf_tensor("p5wf", [128, 8 * D], BF16))
            xe_sb = E2(nc.sbuf_tensor("p5xe", [128, 8 * (CH + 3)], BF16))
            iext = E2(nc.sbuf_tensor("p5ie", [128, 520], F32))
            fext = E2(nc.sbuf_tensor("p5fe", [128, 520], F32))
            cvb_i = E2(nc.sbuf_tensor("p5ci", [128, 512], F32))
            cvb_f = E2(nc.sbuf_tensor("p5cf", [128, 512], F32))
            fg = E2(nc.sbuf_tensor("p5fg", [128, 512], F32))
            sil = E2(nc.sbuf_tensor("p5si", [128, 512], F32))
            isg = E2(nc.sbuf_tensor("p5is", [128, 512], F32))
            hout_pair = [E2(nc.sbuf_tensor(f"p5ho{i}", [128, 8 * 512], F32)) for i in range(2)]
            car = E2(nc.sbuf_tensor("p5ca", [128, 16], F32))
            prs = E2(nc.psum_tensor("p5pr", [128, 520], F32))

            def _(g):
                g.dma_start(out=wi_sb[:].rearrange("p (a d) -> p a d", a=8),
                            in_=wi_d.rearrange("(a p) d -> p a d", p=128)).then_inc(gld.h, 16)
                g.dma_start(out=wf_sb[:].rearrange("p (a d) -> p a d", a=8),
                            in_=wf_d.rearrange("(a p) d -> p a d", p=128)).then_inc(gld.h, 16)
                gld.n += 32
                g.memset(car[:, 0:8], 0.0)
                gm.tick(g.engine_nop())
            block.gpsimd(_)
            p5_ready = gld.n
            p5_carz = gm.n

            def _(v, p5_ready=p5_ready, p5_carz=p5_carz):
                v.wait_ge(gld.h, p5_ready)
                v.wait_ge(gm.h, p5_carz)
                sv.tick(v.engine_nop())
            block.vector(_)

            def _(t, p5_ready=p5_ready):
                t.wait_ge(gld.h, p5_ready)
            block.tensor(_)

            bufs = dict(wi_sb=wi_sb, wf_sb=wf_sb, xe_sb=xe_sb, cvi=cvi, cvf=cvf,
                        iext=iext, fext=fext, cvb_i=cvb_i, cvb_f=cvb_f,
                        fg=fg, sil=sil, isg=isg, zz=zz, hout=hout_pair[0],
                        hcar=car[:, 0:8], pcar=car[:, 8:16], prs=prs,
                        raw_mm=[sm.n], raw_ev=[sv.n])
            hst_mark = {}
            for ct in range(4):
                def _(g, ct=ct):
                    g.wait_ge(sm.h, sm.n)
                    if ct == 0:
                        g.memset(xe_sb[:].rearrange("p (a t) -> p a t", a=8)[:, :, 0:3], 0.0)
                        ins = g.dma_start(
                            out=xe_sb[:].rearrange("p (a t) -> p a t", a=8)[:, :, 3:CH + 3],
                            in_=d_xn1T[:, 0:CH].rearrange("(a p) t -> p a t", p=128))
                    else:
                        ins = g.dma_start(
                            out=xe_sb[:].rearrange("p (a t) -> p a t", a=8),
                            in_=d_xn1T[:, ct * CH - 3:(ct + 1) * CH].rearrange("(a p) t -> p a t", p=128))
                    gdy.tick(ins, 16)
                    gm.tick(g.engine_nop())
                block.gpsimd(_)
                xe_ready = gdy.n
                xe_rdy2 = gm.n

                def _(t, xe_ready=xe_ready):
                    t.wait_ge(gdy.h, xe_ready)
                block.tensor(_)

                def _(v, xe_rdy2=xe_rdy2, xe_ready=xe_ready):
                    v.wait_ge(gdy.h, xe_ready)
                    v.wait_ge(gm.h, xe_rdy2)
                block.vector(_)

                for s_i in range(4):
                    gi = ct * 4 + s_i
                    q = gi % 2
                    bufs["hout"] = hout_pair[q]

                    def _(v, q=q, w=hst_mark.get(gi - 2)):
                        if w is not None:
                            v.wait_ge(sst[q].h, w)
                    block.vector(_)
                    emit_hgrn_sub(block, nc, (sm, sv, ss), bufs, s_i, want_p=False)
                    scan_done = sv.n

                    def _(s_, ct=ct, s_i=s_i, q=q, scan_done=scan_done):
                        s_.wait_ge(sv.h, scan_done)
                        tb = ct * CH + s_i * 512
                        sst[q].tick(s_.dma_start(
                            out=d_h[:, tb:tb + 512].rearrange("(a p) t -> p a t", p=128),
                            in_=hout_pair[q][:].rearrange("p (a t) -> p a t", a=8)), 16)
                    block.sync(_)
                    hst_mark[gi] = sst[q].n
            def _(v):
                v.wait_ge(sst[0].h, sst[0].n)
                v.wait_ge(sst[1].h, sst[1].n)
            block.vector(_)
            barrier()

        def _(g):
            tok = g.snap(own_reg["tok"] + 0, min_val=0, max_val=TSEQ - CH)
            for pc in range(2):
                g.dma_start(out=d_oxn[pc * 512:(pc + 1) * 512, :],
                            in_=d_xn1T[pc * 512:(pc + 1) * 512, bass.ds(tok, CH)]
                            ).then_inc(gdy.h, 16)
            for pc in range(4):
                g.dma_start(out=d_oh[pc * 256:(pc + 1) * 256, :],
                            in_=d_h[pc * 256:(pc + 1) * 256, bass.ds(tok, CH)]
                            ).then_inc(gdy.h, 16)
            gdy.n += 96
            tokd = g.snap(own_reg["tok"] * D, min_val=0, max_val=(TSEQ - CH) * D)
            gdy.tick(g.dma_start(out=d_ox1.rearrange("t d -> (t d)"),
                                 in_=d_x1.rearrange("t d -> (t d)")[bass.ds(tokd, CH * D)]), 16)
            tok8 = g.snap(own_reg["tok"] * 8, min_val=0, max_val=(TSEQ - CH) * 8)
            gdy.tick(g.dma_start(out=d_olg.rearrange("t e -> (t e)"),
                                 in_=d_lg.rearrange("t e -> (t e)")[bass.ds(tok8, CH * 8)]), 16)
        block.gpsimd(_)
        barrier()

        x15_sb = E(nc.sbuf_tensor("x15_sb", [128, 16 * D], F32))
        with contextlib.ExitStack() as c2:
            E2 = c2.enter_context
            wg_sb = E2(nc.sbuf_tensor("p6wg", [128, 8 * D], BF16))
            wo2_sb = E2(nc.sbuf_tensor("p6wo", [128, 8 * D], BF16))
            xnog = E2(nc.sbuf_tensor("p6xn", [128, 8 * 512], BF16))
            hbuf = E2(nc.sbuf_tensor("p6hb", [128, 8 * 512], F32))
            silug = E2(nc.sbuf_tensor("p6sg", [128, 8 * 512], BF16))
            oT = E2(nc.sbuf_tensor("p6oT", [128, 8 * 512], BF16))
            sqa = E2(nc.sbuf_tensor("p6sa", [128, 512], F32))
            sqt = E2(nc.sbuf_tensor("p6st", [128, 512], F32))
            rrow = E2(nc.sbuf_tensor("p6rr", [1, 512], F32))
            rbc = E2(nc.sbuf_tensor("p6rb", [128, 512], F32))
            x1o = E2(nc.sbuf_tensor("p6x1", [128, D], F32))
            gps = E2(nc.psum_tensor("p6pg", [128, 512], F32))
            rsp = E2(nc.psum_tensor("p6ps", [1, 512], F32))
            bps = E2(nc.psum_tensor("p6pb", [128, 512], F32))
            wops = E2(nc.psum_tensor("p6pw", [128, 1024], F32))

            def _(g):
                g.dma_start(out=wg_sb[:].rearrange("p (a d) -> p a d", a=8),
                            in_=wg_d.rearrange("(a p) d -> p a d", p=128)).then_inc(gld.h, 16)
                g.dma_start(out=wo2_sb[:].rearrange("p (a d) -> p a d", a=8),
                            in_=wo_d.rearrange("(a p) d -> p a d", p=128)).then_inc(gld.h, 16)
                gld.n += 32
            block.gpsimd(_)
            p6_ready = gld.n

            xo_mark = [sv.n]
            for s_i in range(4):
                def _(g, s_i=s_i):
                    g.wait_ge(sm.h, sm.n)
                    toff = g.snap(own_reg["tok"] + s_i * 512, min_val=0, max_val=TSEQ - 512)
                    g.dma_start(out=xnog[:].rearrange("p (a t) -> p a t", a=8),
                                in_=d_xn1T.rearrange("(a p) t -> p a t", p=128)[:, :, bass.ds(toff, 512)]
                                ).then_inc(gdy.h, 16)
                    g.dma_start(out=hbuf[:].rearrange("p (a t) -> p a t", a=8)[:, 0:4, :],
                                in_=d_h.rearrange("(a p) t -> p a t", p=128)[:, 0:4, bass.ds(toff, 512)]
                                ).then_inc(gdy.h, 16)
                    ins = g.dma_start(out=hbuf[:].rearrange("p (a t) -> p a t", a=8)[:, 4:8, :],
                                      in_=d_h.rearrange("(a p) t -> p a t", p=128)[:, 4:8, bass.ds(toff, 512)])
                    gdy.n += 32
                    gdy.tick(ins, 16)
                block.gpsimd(_)
                p6_in = gdy.n

                gsil = {}
                for dt in range(8):
                    def _(t, s_i=s_i, dt=dt, p6_in=p6_in, p6_ready=p6_ready):
                        t.wait_ge(gdy.h, p6_in)
                        t.wait_ge(gld.h, p6_ready)
                        if dt > 0:
                            t.wait_ge(ss.h, gsil[dt - 1])
                        elif s_i > 0:
                            t.wait_ge(sv.h, xo_mark[0])
                        for kt in range(8):
                            ins = t.matmul(gps[:, :],
                                           wg_sb[:, kt * 1024 + dt * 128: kt * 1024 + (dt + 1) * 128],
                                           xnog[:, kt * 512:(kt + 1) * 512],
                                           start=(kt == 0), stop=(kt == 7))
                        sm.tick(ins)
                    block.tensor(_)
                    gmk = sm.n

                    def _(s_, dt=dt, gmk=gmk):
                        s_.wait_ge(sm.h, gmk)
                        gsil[dt] = ss.tick(s_.activation(silug[:, dt * 512:(dt + 1) * 512],
                                                         gps[:, :], AF.Silu))
                    block.scalar(_)

                def _(v, s_i=s_i, p6_in=p6_in):
                    v.wait_ge(gdy.h, p6_in)
                    v.tensor_tensor(sqa[:], hbuf[:, 0:512], hbuf[:, 0:512], ALU.mult)
                    for dt in range(1, 8):
                        v.tensor_tensor(sqt[:], hbuf[:, dt * 512:(dt + 1) * 512],
                                        hbuf[:, dt * 512:(dt + 1) * 512], ALU.mult)
                        v.drain()
                        v.tensor_tensor(sqa[:], sqa[:], sqt[:], ALU.add)
                        v.drain()
                    sv.tick(v.engine_nop())
                block.vector(_)
                sq_done = sv.n

                def _(t, sq_done=sq_done):
                    t.wait_ge(sv.h, sq_done)
                    sm.tick(t.matmul(rsp[0:1, :], onesc[:, 0:1], sqa[:], start=True, stop=True))
                block.tensor(_)
                rs_mm = sm.n

                def _(v, rs_mm=rs_mm):
                    v.wait_ge(sm.h, rs_mm)
                    v.tensor_scalar(rrow[0:1, :], rsp[0:1, :], INV_D, LN_EPS, ALU.mult, ALU.add)
                    v.drain()
                    sv.tick(v.engine_nop())
                block.vector(_)
                rr1 = sv.n

                def _(s_, rr1=rr1):
                    s_.wait_ge(sv.h, rr1)
                    s_.activation(rrow[0:1, :], rrow[0:1, :], AF.Ln)
                    s_.drain()
                    ss.tick(s_.activation(rrow[0:1, :], rrow[0:1, :], AF.Exp, scale=-0.5))
                block.scalar(_)
                rr2 = ss.n

                def _(t, rr2=rr2):
                    t.wait_ge(ss.h, rr2)
                    sm.tick(t.matmul(bps[:, :], onesf[0:1, :], rrow[0:1, :], start=True, stop=True))
                block.tensor(_)
                bc_mm = sm.n

                def _(v, s_i=s_i, bc_mm=bc_mm):
                    v.wait_ge(sm.h, bc_mm)
                    v.wait_ge(ss.h, gsil[7])
                    v.tensor_copy(rbc[:], bps[:, :])
                    v.drain()
                    for dt in range(8):
                        v.tensor_tensor(sqt[:], hbuf[:, dt * 512:(dt + 1) * 512], rbc[:], ALU.mult)
                        v.drain()
                        v.scalar_tensor_tensor(oT[:, dt * 512:(dt + 1) * 512], sqt[:],
                                               gnw[:, dt:dt + 1], silug[:, dt * 512:(dt + 1) * 512],
                                               ALU.mult, ALU.mult)
                        v.drain()
                    sv.tick(v.engine_nop())
                block.vector(_)
                o_done = sv.n

                for mt in range(4):
                    def _(g, s_i=s_i, mt=mt):
                        if not (s_i == 0 and mt == 0):
                            g.wait_ge(sv.h, xo_mark[0])
                        gt = s_i * 4 + mt
                        sbb.tick(g.dma_start(
                            out=x1o[:], in_=d_ox1[gt * 128:(gt + 1) * 128, :]), 16)
                    block.gpsimd(_)
                    x1_ld = sbb.n

                    def _(t, mt=mt, o_done=o_done, s_i=s_i):
                        t.wait_ge(sv.h, o_done)
                        if not (s_i == 0 and mt == 0):
                            t.wait_ge(sv.h, xo_mark[0])
                        for nh in range(2):
                            for kt in range(8):
                                ins = t.matmul(wops[:, nh * 512:(nh + 1) * 512],
                                               oT[:, kt * 512 + mt * 128: kt * 512 + (mt + 1) * 128],
                                               wo2_sb[:, kt * 1024 + nh * 512: kt * 1024 + (nh + 1) * 512],
                                               start=(kt == 0), stop=(kt == 7))
                        sm.tick(ins)
                    block.tensor(_)
                    wo_mm = sm.n

                    def _(v, s_i=s_i, mt=mt, wo_mm=wo_mm, x1_ld=x1_ld):
                        v.wait_ge(sm.h, wo_mm)
                        v.wait_ge(sbb.h, x1_ld)
                        gt = s_i * 4 + mt
                        xo_mark[0] = sv.tick(v.tensor_tensor(x15_sb[:, gt * D:(gt + 1) * D],
                                                             x1o[:], wops[:], ALU.add))
                    block.vector(_)
            barrier()

        with contextlib.ExitStack() as c2:
            E2 = c2.enter_context
            act_sb = E2(nc.sbuf_tensor("p7ac", [128, 32 * 1024], BF16))
            xn_g = E2(nc.sbuf_tensor("p7xg", [128, 8 * 1024], BF16))
            lg_sb = E2(nc.sbuf_tensor("p7lg", [128, 16 * 8], F32))
            msk = E2(nc.sbuf_tensor("p7mk", [128, 16 * 8], F32))
            mxc = E2(nc.sbuf_tensor("p7mx", [128, 2], F32))
            b1_sb = E2(nc.sbuf_tensor("p7b1", [128, 64], F32))
            b2_sb = E2(nc.sbuf_tensor("p7b2", [1, D], BF16))
            sgt = [E2(nc.sbuf_tensor(f"p7sg{i}", [128, 512], BF16)) for i in range(2)]
            w1a = [E2(nc.sbuf_tensor(f"p7wa{i}", [128, 8 * 512], BF16)) for i in range(2)]
            w1g = [E2(nc.sbuf_tensor(f"p7wg{i}", [128, 8 * 512], BF16)) for i in range(2)]
            w2t = [E2(nc.sbuf_tensor(f"p7w2{i}", [128, D], BF16)) for i in range(2)]

            def _(g):
                g.wait_ge(sv.h, sv.n)
                gld.tick(g.dma_start(
                    out=lg_sb[:].rearrange("p (a e) -> p a e", a=16),
                    in_=d_olg.rearrange("(a p) e -> p a e", p=128)), 16)
            block.gpsimd(_)
            lg_ready = gld.n



            def _(v, lg_ready=lg_ready):
                v.wait_ge(gld.h, lg_ready)
                for mt in range(16):
                    v.reduce_max(mxc[:, 0:1], lg_sb[:, mt * 8:(mt + 1) * 8], axis=AX.X)
                    v.memset(mxc[:, 1:2], 1.0)
                    v.drain()
                    for e in range(EXP):
                        v.tensor_tensor(msk[:, mt * 8 + e: mt * 8 + e + 1],
                                        lg_sb[:, mt * 8 + e: mt * 8 + e + 1], mxc[:, 0:1], ALU.is_equal)
                        v.drain()
                        v.tensor_tensor(msk[:, mt * 8 + e: mt * 8 + e + 1],
                                        msk[:, mt * 8 + e: mt * 8 + e + 1], mxc[:, 1:2], ALU.mult)
                        v.drain()
                        v.tensor_tensor(mxc[:, 1:2], mxc[:, 1:2],
                                        msk[:, mt * 8 + e: mt * 8 + e + 1], ALU.subtract)
                        v.drain()
                sv.tick(v.engine_nop())
            block.vector(_)
            msk_done = sv.n

            yev = [sv.n]
            prevB_mm = [sm.n]
            p7lastA = [None]
            for g_i in range(2):
                def _(g, g_i=g_i, pA=p7lastA[0]):
                    if pA is None:
                        g.wait_ge(sm.h, sm.n)
                    else:
                        g.wait_ge(sm.h, pA)
                    gdy.tick(g.dma_start(
                        out=xn_g[:].rearrange("p (a t) -> p a t", a=8),
                        in_=d_oxn[:, g_i * 1024:(g_i + 1) * 1024]
                        .rearrange("(a p) t -> p a t", p=128)), 16)
                block.gpsimd(_)
                xn_ready = gdy.n

                for e in range(EXP):
                    def _(g, e=e):
                        if not (e == 0 and g_i == 0):
                            g.wait_ge(sv.h, yev[0])
                        sbb.tick(g.dma_start(out=b1_sb[:], in_=b1_d[e, :, :]), 16)
                        sbb.tick(g.dma_start(out=b2_sb[0:1, :], in_=b2_d[e, 0:1, :]), 16)
                    block.gpsimd(_)
                    be_ready = sbb.n

                    with contextlib.ExitStack() as c3:
                        E3 = c3.enter_context
                        psa = [E3(nc.psum_tensor(f"p7pa{e}{g_i}{i}", [128, 512], F32)) for i in range(2)]
                        psg = [E3(nc.psum_tensor(f"p7pg{e}{g_i}{i}", [128, 512], F32)) for i in range(2)]
                        va = []
                        wld = {}
                        hm = []
                        for fb in range(8):
                            ppw = fb % 2

                            def _(g, e=e, fb=fb, ppw=ppw):
                                if fb >= 2:
                                    g.wait_ge(sm.h, hm[(fb - 1) * 8 - 1])
                                sw1[ppw].tick(g.dma_start(
                                    out=w1a[ppw][:].rearrange("p (a m) -> p a m", a=8),
                                    in_=w1_d[e, :, fb * 512:(fb + 1) * 512]
                                    .rearrange("(a p) m -> p a m", p=128)), 16)
                                wld[fb] = sw1[ppw].tick(g.dma_start(
                                    out=w1g[ppw][:].rearrange("p (a m) -> p a m", a=8),
                                    in_=w1_d[e, :, DFF + fb * 512: DFF + (fb + 1) * 512]
                                    .rearrange("(a p) m -> p a m", p=128)), 16)
                            block.gpsimd(_)

                            for ftl in range(4):
                                ft = fb * 4 + ftl
                                for nch in range(2):
                                    li = fb * 8 + ftl * 2 + nch
                                    pp = li % 2

                                    def _(t, fb=fb, ftl=ftl, pp=pp, ppw=ppw, li=li,
                                          be_ready=be_ready, xn_ready=xn_ready, nch=nch,
                                          msk_done=msk_done):
                                        t.wait_ge(sw1[ppw].h, wld[fb])
                                        t.wait_ge(sbb.h, be_ready)
                                        t.wait_ge(gdy.h, xn_ready)
                                        t.wait_ge(sv.h, msk_done)
                                        if li >= 2:
                                            t.wait_ge(sv.h, va[li - 2])
                                        else:
                                            t.wait_ge(sv.h, yev[0])
                                        rhs0 = nch * 512
                                        for kt in range(8):
                                            t.matmul(psg[pp][:, :],
                                                     w1g[ppw][:, kt * 512 + ftl * 128: kt * 512 + (ftl + 1) * 128],
                                                     xn_g[:, kt * 1024 + rhs0: kt * 1024 + rhs0 + 512],
                                                     start=(kt == 0), stop=(kt == 7))
                                        for kt in range(8):
                                            ins = t.matmul(psa[pp][:, :],
                                                           w1a[ppw][:, kt * 512 + ftl * 128: kt * 512 + (ftl + 1) * 128],
                                                           xn_g[:, kt * 1024 + rhs0: kt * 1024 + rhs0 + 512],
                                                           start=(kt == 0), stop=(kt == 7))
                                        hm.append(sm.tick(ins))
                                    block.tensor(_)

                                    def _(s_, pp=pp, ft=ft, li=li):
                                        s_.wait_ge(sm.h, hm[li])
                                        if li >= 2:
                                            s_.wait_ge(sv.h, va[li - 2])
                                        ss.tick(s_.activation(sgt[pp][:], psg[pp][:, :], AF.Sigmoid,
                                                              bias=b1_sb[:, 32 + ft: 33 + ft]))
                                    block.scalar(_)
                                    sgm = ss.n

                                    def _(v, pp=pp, ft=ft, nch=nch, sgm=sgm, prevB_mm=prevB_mm):
                                        v.wait_ge(ss.h, sgm)
                                        v.wait_ge(sm.h, prevB_mm[0])
                                        ins = v.scalar_tensor_tensor(
                                            act_sb[:, ft * 1024 + nch * 512: ft * 1024 + (nch + 1) * 512],
                                            psa[pp][:, :], b1_sb[:, ft:ft + 1], sgt[pp][:],
                                            ALU.add, ALU.mult)
                                        va.append(sv.tick(ins))
                                    block.vector(_)

                    p7lastA[0] = hm[-1]
                    act_ready = sv.n
                    with contextlib.ExitStack() as c3:
                        E3 = c3.enter_context
                        yps = [E3(nc.psum_tensor(f"p7y{e}{g_i}{i}", [128, 1024], F32)) for i in range(4)]
                        for tg in range(2):
                            w2m = {}
                            mmk = {}
                            yev_now = yev[0]
                            tg_prev_mm = sm.n
                            for kt in range(33):
                                if kt < 32:
                                    ppw = kt % 2

                                    def _(g, e=e, kt=kt, ppw=ppw, tg_prev_mm=tg_prev_mm):
                                        if kt >= 2:
                                            g.wait_ge(sm.h, mmk[kt - 2])
                                        else:
                                            g.wait_ge(sm.h, tg_prev_mm)
                                        w2m[kt] = sw2[ppw].tick(g.dma_start(
                                            out=w2t[ppw][:], in_=w2_d[e, kt * 128:(kt + 1) * 128, :]), 16)
                                    block.gpsimd(_)

                                def _(t, tg=tg, kt=kt, yev_now=yev_now, act_ready=act_ready):
                                    ins = None
                                    if kt < 32:
                                        t.wait_ge(sw2[kt % 2].h, w2m[kt])
                                        t.wait_ge(sv.h, act_ready)
                                        if kt == 0:
                                            t.wait_ge(sv.h, yev_now)
                                        for mt4 in range(4):
                                            for nh in range(2):
                                                ins = t.matmul(
                                                    yps[mt4][:, nh * 512:(nh + 1) * 512],
                                                    act_sb[:, kt * 1024 + tg * 512 + mt4 * 128:
                                                           kt * 1024 + tg * 512 + (mt4 + 1) * 128],
                                                    w2t[kt % 2][:, nh * 512:(nh + 1) * 512],
                                                    start=(kt == 0), stop=False)
                                    else:
                                        for mt4 in range(4):
                                            for nh in range(2):
                                                ins = t.matmul(
                                                    yps[mt4][:, nh * 512:(nh + 1) * 512],
                                                    onesb[0:1, 0:128],
                                                    b2_sb[0:1, nh * 512:(nh + 1) * 512],
                                                    start=False, stop=True)
                                    mmk[kt] = sm.tick(ins)
                                block.tensor(_)
                            ydone = sm.n

                            def _(v, e=e, g_i=g_i, tg=tg, ydone=ydone):
                                v.wait_ge(sm.h, ydone)
                                for mt4 in range(4):
                                    gt = g_i * 8 + tg * 4 + mt4
                                    ins = v.scalar_tensor_tensor(
                                        x15_sb[:, gt * D:(gt + 1) * D], yps[mt4][:],
                                        msk[:, gt * 8 + e: gt * 8 + e + 1],
                                        x15_sb[:, gt * D:(gt + 1) * D], ALU.mult, ALU.add)
                                sv.tick(ins)
                            block.vector(_)
                            yev[0] = sv.n
                    prevB_mm[0] = sm.n

            def _(s_):
                s_.wait_ge(sv.h, sv.n)
                sy.tick(s_.dma_start(out=x2_d.rearrange("(a p) d -> p a d", p=128),
                                     in_=x15_sb[:].rearrange("p (a d) -> p a d", a=16)), 16)
                s_.wait_ge(sy.h, sy.n)
            block.sync(_)
    return nc



_NC_CACHE = {}


def _nc(name, builder):
    if name not in _NC_CACHE:
        _NC_CACHE[name] = builder()
    return _NC_CACHE[name]


def _c(a):
    return np.ascontiguousarray(a)


def _run(name, builder, in_maps):
    nc = _nc(name, builder)
    res = run_bass_kernel_spmd(nc, in_maps, core_ids=list(range(NCORE)))
    return res.results


def kernel_multi(**inputs):
    x = np.asarray(inputs["L0_attn_x"], dtype=np.float32)
    xflat = _c(x.reshape(TOK, D))
    ident = np.eye(128, dtype=np.float32)

    r1 = _run("b1", build_b1,
              [dict(x0c=xflat[c * CH:(c + 1) * CH], ident=ident) for c in range(NCORE)])
    xnT_list = [r1[c]["xnT"] for c in range(NCORE)]
    xnT_full = _c(np.concatenate(xnT_list, axis=1))

    Wqkv = np.asarray(inputs["L0_attn_Wqkv"], dtype=np.float32)
    wq_full, wk_full, wv_full = Wqkv[0:D], Wqkv[D:2 * D], Wqkv[2 * D:3 * D]
    jj = np.arange(512)
    pp_ = np.arange(128)[:, None]
    tri = np.zeros((128, 4 * 512), dtype=np.float32)
    for v in range(4):
        tri[:, v * 512:(v + 1) * 512] = (jj[None, :] >= (pp_ + 128 * v)).astype(np.float32)
    maps = []
    for c in range(NCORE):
        maps.append(dict(
            xnT_full=xnT_full,
            wqT=_c(wq_full[c * 128:(c + 1) * 128, :].T),
            wkT=_c(wk_full[c * 128:(c + 1) * 128, :].T),
            wvT=_c(wv_full[c * 128:(c + 1) * 128, :].T),
            tri=tri, ident=ident))
    r2 = _run("b2", build_b2, maps)
    aoT_list = [r2[c]["aoT"] for c in range(NCORE)]

    Wout = np.asarray(inputs["L0_attn_Wout"], dtype=np.float32)
    woutT = _c(Wout.T)
    W1 = np.asarray(inputs["L0_ffn_W1"], dtype=np.float32)
    W2 = np.asarray(inputs["L0_ffn_W2"], dtype=np.float32)

    def col32(vec):
        return _c(np.asarray(vec, dtype=np.float32).reshape(32, 128).T)

    b1v = col32(inputs["L0_ffn_b1"])
    thv = col32(inputs["L0_ffn_quantum_phases"])
    wrv = col32(inputs["L0_ffn_quantum_weights_real"])
    wiv = col32(inputs["L0_ffn_quantum_weights_imag"])
    b2row = _c(np.asarray(inputs["L0_ffn_b2"], dtype=np.float32).reshape(1, D))
    rw = np.asarray(inputs["L1_ffn_router_w"], dtype=np.float32)
    rbrow = _c(np.asarray(inputs["L1_ffn_router_b"], dtype=np.float32).reshape(1, 8))
    maps = []
    for c in range(NCORE):
        aoTc = _c(np.concatenate([aoT_list[j][:, c * CH:(c + 1) * CH] for j in range(NCORE)], axis=0))
        maps.append(dict(
            x0c=xflat[c * CH:(c + 1) * CH], aoTc=aoTc, woutT=woutT,
            xnTc=_c(xnT_list[c]), W1=W1, W2=W2,
            b1v=b1v, thv=thv, wrv=wrv, wiv=wiv, b2row=b2row,
            rw=rw, rbrow=rbrow, ident=ident))
    r3 = _run("b3", build_b3, maps)
    x1_list = [r3[c]["x1c"] for c in range(NCORE)]
    xn1T_list = [r3[c]["xn1T"] for c in range(NCORE)]
    logit_list = [r3[c]["logits"] for c in range(NCORE)]

    Wi = np.asarray(inputs["L1_attn_Wi"], dtype=np.float32)
    Wf = np.asarray(inputs["L1_attn_Wf"], dtype=np.float32)
    Wg = np.asarray(inputs["L1_attn_Wg"], dtype=np.float32)
    Wo = np.asarray(inputs["L1_attn_Wo"], dtype=np.float32)

    def conv32(a):
        return _c(np.asarray(a, dtype=np.float32).reshape(8, 128, 4).transpose(1, 0, 2).reshape(128, 32))

    cvi = conv32(inputs["L1_attn_conv_i"])
    cvf = conv32(inputs["L1_attn_conv_f"])
    gnw = _c(np.asarray(inputs["L1_attn_g_norm_weight"], dtype=np.float32).reshape(8, 128).T)
    xe_list = []
    for c in range(NCORE):
        if c % 4 == 0:
            prev = np.zeros((D, 3), dtype=BF)
        else:
            prev = xn1T_list[c - 1][:, -3:]
        xe_list.append(_c(np.concatenate([prev, xn1T_list[c]], axis=1)))
    r4 = _run("b4", build_b4,
              [dict(xn1ext=xe_list[c], wi=Wi, wf=Wf, cvi=cvi, cvf=cvf) for c in range(NCORE)])
    car_list = [r4[c]["carries"] for c in range(NCORE)]

    carsall = _c(np.stack(car_list, axis=0).transpose(1, 0, 2).reshape(128, 128))
    maps = []
    for c in range(NCORE):
        fl = np.zeros(8, dtype=np.float32)
        for j in range(NCORE):
            if j < c and j // 4 == c // 4:
                fl[j] = 1.0
        selfl = _c(np.broadcast_to(fl, (128, 8)))
        maps.append(dict(xn1ext=xe_list[c], wi=Wi, wf=Wf, wg=Wg, wo=Wo,
                         cvi=cvi, cvf=cvf, gnw=gnw, carsall=carsall, selfl=selfl,
                         x1c=x1_list[c]))
    r5 = _run("b5", build_b5, maps)
    x15_list = [r5[c]["x15c"] for c in range(NCORE)]

    w1 = np.asarray(inputs["L1_ffn_w1"], dtype=np.float32)
    b1m = np.asarray(inputs["L1_ffn_b1"], dtype=np.float32)
    b1 = _c(b1m.reshape(EXP, 64, 128).transpose(0, 2, 1))
    w2 = np.asarray(inputs["L1_ffn_w2"], dtype=np.float32)
    b2 = _c(np.asarray(inputs["L1_ffn_b2"], dtype=np.float32).reshape(EXP, 1, D))
    maps = []
    for c in range(NCORE):
        maps.append(dict(xn1Tc=_c(xn1T_list[c]), x15c=x15_list[c],
                         logits=logit_list[c], w1=w1, b1=b1, w2=w2, b2=b2))
    r6 = _run("b6", build_b6, maps)
    out = np.concatenate([r6[c]["x2c"] for c in range(NCORE)], axis=0)
    return out.reshape(NB, TSEQ, D).astype(np.float32)


def kernel(**inputs):
    x = np.asarray(inputs["L0_attn_x"], dtype=np.float32)
    ident = np.eye(128, dtype=np.float32)
    jj = np.arange(512)
    pp_ = np.arange(128)[:, None]
    tri = np.zeros((128, 4 * 512), dtype=np.float32)
    for v in range(4):
        tri[:, v * 512:(v + 1) * 512] = (jj[None, :] >= (pp_ + 128 * v)).astype(np.float32)

    def col32(vec):
        return _c(np.asarray(vec, dtype=np.float32).reshape(32, 128).T)

    def conv32(a):
        return _c(np.asarray(a, dtype=np.float32).reshape(8, 128, 4).transpose(1, 0, 2).reshape(128, 32))

    selm = np.zeros((16, 8 * 128), dtype=np.float32)
    for a in range(8):
        selm[2 * a, a * 128:a * 128 + 64] = 1.0
        selm[2 * a + 1, a * 128 + 64:a * 128 + 128] = 1.0
    Wqkv = np.asarray(inputs["L0_attn_Wqkv"], dtype=np.float32)
    shared = dict(
        wqT=_c(Wqkv[0:D].T), wkT=_c(Wqkv[D:2 * D].T), wvT=_c(Wqkv[2 * D:3 * D].T),
        woutT=_c(np.asarray(inputs["L0_attn_Wout"], dtype=np.float32).T),
        W1=np.asarray(inputs["L0_ffn_W1"], dtype=np.float32),
        W2=np.asarray(inputs["L0_ffn_W2"], dtype=np.float32),
        b1v=col32(inputs["L0_ffn_b1"]),
        thv=col32(inputs["L0_ffn_quantum_phases"]),
        wrv=col32(inputs["L0_ffn_quantum_weights_real"]),
        wiv=col32(inputs["L0_ffn_quantum_weights_imag"]),
        b2row=_c(np.asarray(inputs["L0_ffn_b2"], dtype=np.float32).reshape(1, D)),
        rw=np.asarray(inputs["L1_ffn_router_w"], dtype=np.float32),
        rbrow=_c(np.asarray(inputs["L1_ffn_router_b"], dtype=np.float32).reshape(1, 8)),
        wi=np.asarray(inputs["L1_attn_Wi"], dtype=np.float32),
        wf=np.asarray(inputs["L1_attn_Wf"], dtype=np.float32),
        wg=np.asarray(inputs["L1_attn_Wg"], dtype=np.float32),
        wo=np.asarray(inputs["L1_attn_Wo"], dtype=np.float32),
        cvi=conv32(inputs["L1_attn_conv_i"]),
        cvf=conv32(inputs["L1_attn_conv_f"]),
        gnw=_c(np.asarray(inputs["L1_attn_g_norm_weight"], dtype=np.float32).reshape(8, 128).T),
        w1m=np.asarray(inputs["L1_ffn_w1"], dtype=np.float32),
        b1m=_c(np.asarray(inputs["L1_ffn_b1"], dtype=np.float32).reshape(EXP, 64, 128).transpose(0, 2, 1)),
        w2m=np.asarray(inputs["L1_ffn_w2"], dtype=np.float32),
        b2m=_c(np.asarray(inputs["L1_ffn_b2"], dtype=np.float32).reshape(EXP, 1, D)),
        tri=tri, ident=ident, selb=selm)

    maps = []
    for c in range(NCORE):
        b, j = c // 4, c % 4
        m = dict(shared)
        m["xb"] = _c(x[b])
        m["owno"] = np.array([[j * CH, j * 16]], dtype=np.int32)
        maps.append(m)
    res = _run("one", build_one, maps)
    out = np.empty((NB, TSEQ, D), dtype=np.float32)
    for c in range(NCORE):
        b, j = c // 4, c % 4
        out[b, j * CH:(j + 1) * CH, :] = res[c]["x2c"]
    return out
